# version 06
# baseline (speedup 1.0000x reference)
import numpy as np
import ml_dtypes
from contextlib import ExitStack

import concourse.bass as bass
import concourse.mybir as mybir
from concourse.bass_utils import run_bass_kernel_spmd

F32 = mybir.dt.float32
BF16 = mybir.dt.bfloat16
AF = mybir.ActivationFunctionType
ALU = mybir.AluOpType
AX = mybir.AxisListType

SEQ = 16384
D = 1024
NCORES = 8
NT = 16
NBLK = 32
BK = 512
DCH = 8
DH = 4096
NEG = -30000.0

ENGS = ("sync", "vector", "scalar", "tensor", "gpsimd")


class _Em:

    def __init__(self):
        self.ev = {e: [] for e in ENGS}
        self.cnt = {}
        self.gate = {}

    def c(self, s):
        return self.cnt.get(s, 0)

    def add(self, eng, fn, waits=(), inc=None, dma=False):
        n = 16 if dma else 1
        if inc:
            self.cnt[inc] = self.c(inc) + n
        self.ev[eng].append((fn, tuple(w for w in waits if w is not None), inc, n))
        return self.c(inc) if inc else None

    def g(self, key):
        return self.gate.get(key)

    def setg(self, key, sem):
        self.gate[key] = (sem, self.c(sem))


def _parts(j):
    return [t for t in range(NT) if j < 2 * t + 2]


def build_nc(dbg=False, nblk=NBLK, serial=False):
    nc = bass.Bass("TRN2", target_bir_lowering=False, debug=False,
                   detect_race_conditions=False)
    dram = nc.dram_tensor

    xqT = dram("xqT", [D, 2048], F32, kind="ExternalInput").ap()
    cc_src = [dram("ccs%d" % s, [262144], BF16, kind="Internal").ap()
              for s in range(NT)]
    cc_dst = [dram("ccd%d" % s, [8, 262144], BF16, kind="Internal").ap()
              for s in range(NT)]
    umask = dram("umask", [128, 2048], BF16, kind="ExternalInput").ap()
    vmask = dram("vmask", [128, SEQ], BF16, kind="ExternalInput").ap()
    tri = dram("tri", [128, 128], BF16, kind="ExternalInput").ap()
    idn = dram("idn", [128, 128], BF16, kind="ExternalInput").ap()
    wkT = dram("wkT", [D, D], F32, kind="ExternalInput").ap()
    wvT = dram("wvT", [D, D], F32, kind="ExternalInput").ap()
    w1T = dram("w1T", [D, DH], F32, kind="ExternalInput").ap()
    w2T = dram("w2T", [D, DH], F32, kind="ExternalInput").ap()
    h1T = dram("h1T", [DH, DH], F32, kind="ExternalInput").ap()
    h2T = dram("h2T", [DH, DH], F32, kind="ExternalInput").ap()
    w3T = dram("w3T", [DH, D], F32, kind="ExternalInput").ap()
    s1i = dram("s1i", [128, 32], F32, kind="ExternalInput").ap()
    s2i = dram("s2i", [128, 32], F32, kind="ExternalInput").ap()
    sh1i = dram("sh1i", [128, 32], F32, kind="ExternalInput").ap()
    sh2i = dram("sh2i", [128, 32], F32, kind="ExternalInput").ap()
    s3i = dram("s3i", [128, 8], F32, kind="ExternalInput").ap()
    epsi = dram("epsi", [128, 1], F32, kind="ExternalInput").ap()
    out = dram("out", [2048, D], F32, kind="ExternalOutput").ap()
    if dbg:
        dbg_x1 = dram("dbg_x1", [128, DCH * 2048], BF16, kind="ExternalOutput").ap()
        dbg_O = dram("dbg_O", [128, NT * 1024], F32, kind="ExternalOutput").ap()
        dbg_s = dram("dbg_s", [128, NT], F32, kind="ExternalOutput").ap()

    sems = {}
    for s in ("s_in", "s_inw", "s_xk0", "s_xk1", "s_vm0", "s_vm1",
              "s_kvg", "s_kve", "s_S",
              "s_vstat", "s_exp", "s_T", "s_PTc", "s_PV", "s_Oupd",
              "s_ln0", "s_ln1", "s_lnv", "s_lns", "s_cc", "s_pk0", "s_pk1",
              "s_kv0", "s_kv1",
              "s_w0", "s_w1", "s_f", "s_sc", "s_ve", "s_l2T", "s_out", "s_dbg",
              "s_vch", "s_sch"):
        sems[s] = nc.alloc_semaphore(s)

    pa = nc.alloc_psum_tensor
    S_ps = [pa("S0", [128, 512], F32), pa("S1", [128, 512], F32)]
    KV_ps = [pa("KVA", [128, 512], F32), pa("KVB", [128, 512], F32)]
    PV_ps = pa("PVp", [128, 1024], F32)
    TP_ps = pa("TPp", [128, 1024], BF16)
    LT_ps = pa("LTp", [128, 1024], BF16)
    FB = [S_ps[0], S_ps[1], KV_ps[0], KV_ps[1]]

    em = _Em()

    def mm(psl, lhsT, rhs, start, stop):
        return (lambda P_, L_, R_, st_, sp_:
                (lambda e: e.matmul(P_, L_, R_, start=st_, stop=sp_)))(
                    psl, lhsT, rhs, start, stop)

    def cp(dst, src):
        return (lambda D_, S_: (lambda e: e.tensor_copy(D_, S_)))(dst, src)

    def dmaev(eng, dst, src, inc, waits=()):
        em.add(eng, (lambda d, s: (lambda e: e.dma_start(out=d, in_=s)))(dst, src),
               waits=waits, inc=inc, dma=True)

    stk = ExitStack()

    def sb(name, shape, dt, st=stk, side=None):
        kw = {} if side is None else {"side": side}
        return st.enter_context(nc.sbuf_tensor(name, shape, dt, **kw))

    idn_sb = sb("idn_sb", [128, 128], BF16)
    s1_sb = sb("s1_sb", [128, 32], F32)
    s2_sb = sb("s2_sb", [128, 32], F32)
    sh1_sb = sb("sh1_sb", [128, 32], F32)
    sh2_sb = sb("sh2_sb", [128, 32], F32)
    s3_sb = sb("s3_sb", [128, 8], F32)
    eps_sb = sb("eps_sb", [128, 1], F32)

    stkA = ExitStack()
    O_sb = sb("O_sb", [128, NT, 1024], F32, stkA)
    x_fm = sb("x_fm", [128, DCH, 2048], BF16, stkA)
    s_sb = sb("s_sb", [128, NT], F32, stkA)
    nm_sb = sb("nm_sb", [128, NT], F32, stkA)
    nmn_sb = sb("nmn_sb", [128, 2], F32, stkA)
    d_sb = sb("d_sb", [128, 2], F32, stkA)
    ct_sb = sb("ct_sb", [128, 2], F32, stkA)
    sbk_sb = sb("sbk_sb", [128, 2], F32, stkA)
    bmax_sb = sb("bmax_sb", [128, 2], F32, stkA)

    stkB = ExitStack()
    u_sb = sb("u_sb", [128, 2048], BF16, stkB)
    wk_sb = sb("wk_sb", [128, DCH, 1024], BF16, stkB)
    wv_sb = sb("wv_sb", [128, DCH, 1024], BF16, stkB)
    K_sb = [sb("K0", [128, DCH, BK], BF16, stkB), sb("K1", [128, DCH, BK], BF16, stkB)]
    V_sb = [sb("V0", [128, 4, 1024], BF16, stkB), sb("V1", [128, 4, 1024], BF16, stkB)]
    vm_sb = [sb("vm0", [128, BK], BF16, stkB), sb("vm1", [128, BK], BF16, stkB)]
    Kt_sb = [sb("Kt0", [128, DCH, 128], BF16, stkB), sb("Kt1", [128, DCH, 128], BF16, stkB)]
    Vt_sb = [sb("Vt0", [128, 1024], BF16, stkB), sb("Vt1", [128, 1024], BF16, stkB)]
    P_sb = [sb("P0", [128, 512], BF16, stkB), sb("P1", [128, 512], BF16, stkB)]
    PT_sb = [sb("PT0", [128, 512], BF16, stkB), sb("PT1", [128, 512], BF16, stkB)]
    tri_sb = sb("tri_sb", [128, 128], BF16, stkB)

    dmaev("sync", idn_sb[:], idn, "s_in")
    dmaev("sync", tri_sb[:], tri, "s_in")
    dmaev("sync", u_sb[:], umask, "s_in")
    dmaev("sync", s1_sb[:], s1i, "s_in")
    dmaev("sync", s2_sb[:], s2i, "s_in")
    dmaev("sync", sh1_sb[:], sh1i, "s_in")
    dmaev("sync", sh2_sb[:], sh2i, "s_in")
    dmaev("sync", s3_sb[:], s3i, "s_in")
    dmaev("sync", eps_sb[:], epsi, "s_in")
    SETUP_SYNC = em.c("s_in")

    dmaev("gpsimd", x_fm[:], xqT.rearrange("(c p) q -> p c q", p=128), "s_inw")
    dmaev("gpsimd", wk_sb[:], wkT.rearrange("(c p) o -> p c o", p=128), "s_inw")
    dmaev("gpsimd", wv_sb[:], wvT.rearrange("(c p) o -> p c o", p=128), "s_inw")
    SETUP_GP = em.c("s_inw")

    VM_LOADED = {}
    KVE_BLK = {}

    def load_vm(j):
        g = em.g(("vm", j % 2))
        sname = "s_vm%d" % (j % 2)
        dmaev("gpsimd", vm_sb[j % 2][:], vmask[:, j * BK:(j + 1) * BK],
              sname, waits=([g] if g else []))
        VM_LOADED[j] = (sname, em.c(sname))

    def load_kv(j):
        s, cb = j // 2, 4 * (j % 2)
        pr = j % 2
        sname = "s_kv%d" % pr
        gk = em.g(("Kpar", pr))
        gv = em.g(("Vpar", pr))
        w = [("s_cc", s + 1)] + [x for x in (gk, gv) if x]
        dmaev("gpsimd",
              K_sb[pr][:].rearrange("p c k -> p c k"),
              cc_dst[s][cb:cb + 4, 0:131072].rearrange(
                  "t (p c k) -> p c t k", p=128, c=DCH),
              sname, waits=w)
        dmaev("gpsimd",
              V_sb[pr][:],
              cc_dst[s][cb:cb + 4, 131072:262144].rearrange(
                  "t (p v) -> p t v", p=128),
              sname)
        KVE_BLK[j] = (sname, em.c(sname))

    load_vm(0)
    load_vm(1)

    rg8 = [list(range(8))]
    for t in range(NT):
        pr = t % 2
        for g in range(10):
            bank = KV_ps[g % 2]
            if g < 8:
                c8 = g
                lhss = [wk_sb[:, ic, c8 * 128:(c8 + 1) * 128] for ic in range(DCH)]
                rhss = [x_fm[:, ic, t * 128:(t + 1) * 128] for ic in range(DCH)]
                po = bank[:, 0:128]
                dst = Kt_sb[pr][:, c8, :]
                gates = [em.g(("Ktpar", pr)), em.g(("Ktpk", pr))]
            else:
                h = g - 8
                lhss = [x_fm[:, ic, t * 128:(t + 1) * 128] for ic in range(DCH)]
                rhss = [wv_sb[:, ic, h * 512:(h + 1) * 512] for ic in range(DCH)]
                po = bank[:]
                dst = Vt_sb[pr][:, h * 512:(h + 1) * 512]
                gates = [em.g(("Vtpar", pr)), em.g(("Vtpk", pr))]
            for ic in range(DCH):
                w = []
                if ic == 0:
                    w = [("s_inw", SETUP_GP), em.g(("KVps", g % 2))]
                em.add("tensor", mm(po, lhss[ic], rhss[ic], ic == 0, ic == DCH - 1),
                       waits=w, inc=("s_kvg" if ic == DCH - 1 else None))
            w = [("s_kvg", em.c("s_kvg"))] + gates
            em.add("vector", cp(dst, po), waits=w, inc="s_kve")
            em.setg(("KVps", g % 2), "s_kve")
        KV_T = em.c("s_kve")
        pks = "s_pk%d" % pr
        dmaev("gpsimd", cc_src[t][0:131072].rearrange("(p c k) -> p c k",
                                                      p=128, c=DCH),
              Kt_sb[pr][:], pks, waits=[("s_kve", KV_T)])
        dmaev("gpsimd", cc_src[t][131072:262144].rearrange("(p v) -> p v", p=128),
              Vt_sb[pr][:], pks)
        pkv = em.c(pks)
        em.add("gpsimd",
               (lambda S_, D_: (lambda e: e.collective_compute(
                   "AllGather", ALU.bypass, rg8, ins=[S_], outs=[D_])))(
                   cc_src[t], cc_dst[t]),
               waits=[(pks, pkv)], inc="s_cc")
        em.setg(("Ktpk", pr), pks)
        em.setg(("Vtpk", pr), pks)
        for ic in range(DCH + 1):
            w = []
            if ic == 0:
                w = [("s_kve", KV_T), em.g(("Spar", pr)), ("s_in", SETUP_SYNC)]
            if ic < DCH:
                lhsT = x_fm[:, ic, t * 128:(t + 1) * 128]
                rhs = Kt_sb[pr][:, ic, :]
            else:
                lhsT = idn_sb[:]
                rhs = tri_sb[:]
            em.add("tensor", mm(S_ps[pr][:, 0:128], lhsT, rhs,
                                ic == 0, ic == DCH),
                   waits=w, inc=("s_S" if ic == DCH else None))
        em.setg(("Ktpar", pr), "s_S")
        sS = em.c("s_S")
        em.add("vector",
               (lambda P_, M_: (lambda e: e.reduce_max(M_, P_, axis=AX.X,
                                                       negate=True)))(
                   S_ps[pr][:, 0:128], nm_sb[:, t:t + 1]),
               waits=[("s_S", sS)], inc="s_vstat")
        vst = em.c("s_vstat")
        em.add("scalar",
               (lambda Pp, Sp, B_, A_: (lambda e: e.activation(
                   Pp, Sp, AF.Exp, bias=B_, accum_out=A_)))(
                   P_sb[pr][:, 0:128], S_ps[pr][:, 0:128], nm_sb[:, t:t + 1],
                   s_sb[:, t:t + 1]),
               waits=[("s_vstat", vst), em.g(("Ppar", pr))], inc="s_exp")
        em.setg(("Spar", pr), "s_exp")
        sE = em.c("s_exp")
        em.add("tensor",
               (lambda O_, I_, N_: (lambda e: e.transpose(O_, I_, N_)))(
                   TP_ps[:, 0:128], P_sb[pr][:, 0:128], idn_sb[:]),
               waits=[("s_exp", sE), em.g(("TP",)), ("s_in", SETUP_SYNC)],
               inc="s_T")
        em.setg(("Ppar", pr), "s_T")
        em.add("vector", cp(PT_sb[pr][:, 0:128], TP_ps[:, 0:128]),
               waits=[("s_T", em.c("s_T")), em.g(("PTpar", pr))], inc="s_PTc")
        em.setg(("TP",), "s_PTc")
        ptc = em.c("s_PTc")
        for h in range(2):
            em.add("tensor",
                   mm(PV_ps[:, h * 512:(h + 1) * 512], PT_sb[pr][:, 0:128],
                      Vt_sb[pr][:, h * 512:(h + 1) * 512], True, True),
                   waits=([("s_PTc", ptc), em.g(("PVps",))] if h == 0 else ()),
                   inc=("s_PV" if h == 1 else None))
        em.setg(("PTpar", pr), "s_PV")
        em.setg(("Vtpar", pr), "s_PV")
        spv = em.c("s_PV")
        for h in range(2):
            em.add("vector", cp(O_sb[:, t, h * 512:(h + 1) * 512],
                                PV_ps[:, h * 512:(h + 1) * 512]),
                   waits=([("s_PV", spv)] if h == 0 else ()),
                   inc=("s_Oupd" if h == 1 else None))
        em.setg(("PVps",), "s_Oupd")

    def emit_ln1(t, owait, pr):
        zO = zo_sb[pr]
        for c in range(DCH):
            w = [em.g(("LT",))] if c == 0 else []
            em.add("tensor",
                   (lambda O_, I_, N_: (lambda e: e.transpose(O_, I_, N_)))(
                       LT_ps[:, c * 128:(c + 1) * 128],
                       x_fm[:, c, t * 128:(t + 1) * 128], idn_sb[:]),
                   waits=w, inc=("s_lnT" if c == DCH - 1 else None))
        xT_done = em.c("s_lnT")
        em.add("vector",
               (lambda S_, R_: (lambda e: e.reciprocal(R_, S_)))(
                   s_sb[:, t:t + 1], rs_sb[pr][:]),
               waits=owait, inc="s_vch")
        em.add("vector",
               (lambda O_, R_, Z_: (lambda e: e.tensor_scalar_mul(Z_, O_, R_)))(
                   O_sb[:, t, :], rs_sb[pr][:], zO[:]),
               waits=[("s_vch", em.c("s_vch"))])
        em.add("vector",
               (lambda L_, Z_, ZB: (lambda e: e.scalar_tensor_tensor(
                   ZB, L_, 2.0, Z_, ALU.mult, ALU.add)))(
                   LT_ps[:, 0:1024], zO[:], z_sb[pr][:]),
               waits=[("s_lnT", xT_done)], inc="s_lnv")
        em.setg(("LT",), "s_lnv")
        em.add("vector",
               (lambda Z_, M_: (lambda e: e.reduce_sum(M_, Z_, axis=AX.X)))(
                   z_sb[pr][:], mu_sb[pr][:]),
               inc="s_lnv")
        zs = em.c("s_lnv")
        em.add("scalar",
               (lambda M_: (lambda e: e.mul(M_, M_, 1.0 / D)))(mu_sb[pr][:]),
               waits=[("s_lnv", zs)], inc="s_lns")
        em.add("scalar",
               (lambda Q_, Z_, M_, V_: (lambda e: e.activation(
                   Q_, Z_, AF.Square, bias=M_, scale=-1.0, accum_out=V_)))(
                   zo_sb[pr][:], z_sb[pr][:], mu_sb[pr][:], vs_sb[pr][:]))
        em.add("scalar",
               (lambda V_, S_, E_: (lambda e: e.activation(S_, V_, AF.Sqrt,
                                                           bias=E_, scale=1.0 / D)))(
                   vs_sb[pr][:], sd_sb[pr][:], eps_sb[:]),
               waits=[("s_in", SETUP_SYNC)], inc="s_lns")
        em.add("vector",
               (lambda S_, R_: (lambda e: e.reciprocal(R_, S_)))(
                   sd_sb[pr][:], rstd_sb[pr][:]),
               waits=[("s_lns", em.c("s_lns"))], inc="s_vch")
        em.add("vector",
               (lambda Z_, M_, R_, X_: (lambda e: e.tensor_scalar(
                   X_, Z_, M_, R_, ALU.subtract, ALU.mult)))(
                   z_sb[pr][:], mu_sb[pr][:], rstd_sb[pr][:], x1b_sb[pr][:]),
               waits=[("s_vch", em.c("s_vch"))], inc="s_lnv")
        x1done = em.c("s_lnv")
        for c in range(DCH):
            w = [em.g(("TP",)), ("s_lnv", x1done)] if c == 0 else []
            em.add("tensor",
                   (lambda O_, I_, N_: (lambda e: e.transpose(O_, I_, N_)))(
                       TP_ps[:, c * 128:(c + 1) * 128],
                       x1b_sb[pr][:, c * 128:(c + 1) * 128], idn_sb[:]),
                   waits=w, inc=("s_lnT" if c == DCH - 1 else None))
        for half in range(2):
            em.add("vector",
                   (lambda D2_, S2_: (lambda e: e.tensor_copy(D2_, S2_)))(
                       x1_fm[:, half * 4:(half + 1) * 4, t * 128:(t + 1) * 128],
                       TP_ps[:, half * 512:(half + 1) * 512].rearrange(
                           "p (c q) -> p c q", c=4)),
                   waits=([("s_lnT", em.c("s_lnT"))] if half == 0 else ()),
                   inc=("s_lnv" if half == 1 else None))
        em.setg(("TP",), "s_lnv")

    jobq = [(j, t) for j in range(nblk) for t in _parts(j)]
    NJOBS = len(jobq)
    n_start = {}
    n_end = {}
    idx = 0
    for j in range(nblk):
        n_start[j] = idx
        idx += len(_parts(j))
        n_end[j] = idx
    S_OF, VST_OF, EXP_OF, T_OF, PTC_OF, PV_OF, OUP_OF = {}, {}, {}, {}, {}, {}, {}

    if nblk > 0:
        load_kv(0)
    if nblk > 1:
        load_kv(1)

    def emit_S(n):
        j, t = jobq[n]
        pr = j % 2
        bank = S_ps[n % 2]
        for ic in range(DCH + 1):
            if ic == 0:
                w = [KVE_BLK[j], em.g(("Spar", n % 2))]
                if serial:
                    w += [(s_, em.c(s_)) for s_ in
                          ("s_Oupd", "s_exp", "s_PTc", "s_kve", "s_vstat")]
            elif ic == DCH:
                w = [VM_LOADED[j], ("s_in", SETUP_SYNC)]
            else:
                w = []
            if ic < DCH:
                lhsT = x_fm[:, ic, t * 128:(t + 1) * 128]
                rhs = K_sb[pr][:, ic, :]
            else:
                lhsT = u_sb[:, t * 128:(t + 1) * 128]
                rhs = vm_sb[pr][:]
            em.add("tensor", mm(bank[:], lhsT, rhs, ic == 0, ic == DCH),
                   waits=w, inc=("s_S" if ic == DCH else None))
        S_OF[n] = em.c("s_S")
        if n + 1 == n_end[j]:
            em.setg(("Kpar", pr), "s_S")
            em.setg(("vm", pr), "s_S")

    def emit_stats_exp(n):
        j, t = jobq[n]
        q = n % 2
        bank = S_ps[q]
        em.add("vector",
               (lambda B_, M_: (lambda e: e.reduce_max(M_, B_, axis=AX.X,
                                                       negate=True)))(
                   bank[:], bmax_sb[:, q:q + 1]),
               waits=[("s_S", S_OF[n])], inc="s_vch")
        em.add("vector",
               (lambda B_, NM_, D_: (lambda e: e.tensor_scalar(
                   D_, B_, NM_, 0.0, ALU.subtract, ALU.min)))(
                   bmax_sb[:, q:q + 1], nm_sb[:, t:t + 1], d_sb[:, q:q + 1]),
               waits=[("s_vch", em.c("s_vch"))])
        em.add("vector",
               (lambda NM_, D_: (lambda e: e.tensor_scalar_add(NM_, NM_, D_)))(
                   nm_sb[:, t:t + 1], d_sb[:, q:q + 1]),
               inc="s_vstat")
        VST_OF[n] = em.c("s_vstat")
        em.add("scalar",
               (lambda D_, C_: (lambda e: e.activation(C_, D_, AF.Exp)))(
                   d_sb[:, q:q + 1], ct_sb[:, q:q + 1]),
               waits=[("s_vstat", VST_OF[n])])
        em.add("scalar",
               (lambda Pp, Sp, B_, A_: (lambda e: e.activation(
                   Pp, Sp, AF.Exp, bias=B_, accum_out=A_)))(
                   P_sb[q][:], bank[:], nm_sb[:, t:t + 1], sbk_sb[:, q:q + 1]),
               waits=[em.g(("Ppar", q))], inc="s_exp")
        em.setg(("Spar", q), "s_exp")
        EXP_OF[n] = em.c("s_exp")
        em.add("vector",
               (lambda S_, C_, B_: (lambda e: e.scalar_tensor_tensor(
                   S_, S_, C_, B_, ALU.mult, ALU.add)))(
                   s_sb[:, t:t + 1], ct_sb[:, q:q + 1], sbk_sb[:, q:q + 1]),
               waits=[("s_exp", EXP_OF[n])])

    def emit_T(n):
        q = n % 2
        for kc in range(4):
            w = []
            if kc == 0:
                w = [("s_exp", EXP_OF[n]), em.g(("TP",))]
            em.add("tensor",
                   (lambda O_, I_, N_: (lambda e: e.transpose(O_, I_, N_)))(
                       TP_ps[:, kc * 128:(kc + 1) * 128],
                       P_sb[q][:, kc * 128:(kc + 1) * 128], idn_sb[:]),
                   waits=w, inc=("s_T" if kc == 3 else None))
        em.setg(("Ppar", q), "s_T")
        T_OF[n] = em.c("s_T")
        em.add("vector", cp(PT_sb[q][:], TP_ps[:, 0:512]),
               waits=[("s_T", T_OF[n]), em.g(("PTpar", q))], inc="s_PTc")
        em.setg(("TP",), "s_PTc")
        PTC_OF[n] = em.c("s_PTc")

    def emit_PV(n):
        j, t = jobq[n]
        pr = j % 2
        q = n % 2
        for h in range(2):
            for kc in range(4):
                w = []
                if h == 0 and kc == 0:
                    w = [("s_PTc", PTC_OF[n]), em.g(("PVps",))]
                em.add("tensor",
                       mm(PV_ps[:, h * 512:(h + 1) * 512],
                          PT_sb[q][:, kc * 128:(kc + 1) * 128],
                          V_sb[pr][:, kc, h * 512:(h + 1) * 512],
                          kc == 0, kc == 3),
                       waits=w, inc=("s_PV" if (h == 1 and kc == 3) else None))
        em.setg(("PTpar", q), "s_PV")
        if n + 1 == n_end[j]:
            em.setg(("Vpar", pr), "s_PV")
        PV_OF[n] = em.c("s_PV")
        for h in range(2):
            em.add("vector",
                   (lambda O_, C_, Pp: (lambda e: e.scalar_tensor_tensor(
                       O_, O_, C_, Pp, ALU.mult, ALU.add)))(
                       O_sb[:, t, h * 512:(h + 1) * 512], ct_sb[:, q:q + 1],
                       PV_ps[:, h * 512:(h + 1) * 512]),
                   waits=([("s_PV", PV_OF[n])] if h == 0 else ()),
                   inc=("s_Oupd" if h == 1 else None))
        em.setg(("PVps",), "s_Oupd")
        OUP_OF[n] = em.c("s_Oupd")

    emitted = [0]

    def try_emit_S():
        if emitted[0] < NJOBS:
            jn, _ = jobq[emitted[0]]
            if jn in KVE_BLK:
                emit_S(emitted[0])
                emitted[0] += 1

    try_emit_S()
    for n in range(NJOBS):
        j, t = jobq[n]
        emit_stats_exp(n)
        try_emit_S()
        emit_T(n)
        emit_PV(n)
        if n + 1 == n_end[j] and j + 2 < nblk:
            load_kv(j + 2)
            load_vm(j + 2)

    ATT_END = [("s_S", em.c("s_S")), ("s_PV", em.c("s_PV")),
               ("s_Oupd", em.c("s_Oupd")), ("s_exp", em.c("s_exp")),
               ("s_PTc", em.c("s_PTc")), ("s_kve", em.c("s_kve")),
               ("s_vstat", em.c("s_vstat"))]


    stkB.close()
    x1_fm = sb("x1_fm", [128, DCH, 2048], BF16, side="right")
    stkL = ExitStack()
    z_sb = [sb("z0", [128, D], BF16, stkL), sb("z1", [128, D], BF16, stkL)]
    zo_sb = [sb("zo0", [128, D], F32, stkL), sb("zo1", [128, D], F32, stkL)]
    x1b_sb = [sb("x1b0", [128, D], BF16, stkL), sb("x1b1", [128, D], BF16, stkL)]
    rs_sb = [sb("rs0", [128, 1], F32, stkL), sb("rs1", [128, 1], F32, stkL)]
    mu_sb = [sb("mu0", [128, 1], F32, stkL), sb("mu1", [128, 1], F32, stkL)]
    vs_sb = [sb("vs0", [128, 1], F32, stkL), sb("vs1", [128, 1], F32, stkL)]
    sd_sb = [sb("sd0", [128, 1], F32, stkL), sb("sd1", [128, 1], F32, stkL)]
    rstd_sb = [sb("rstd0", [128, 1], F32, stkL), sb("rstd1", [128, 1], F32, stkL)]
    for t in range(NT):
        emit_ln1(t, list(ATT_END) if t == 0 else [], t % 2)
    LN_DONE = [("s_lnv", em.c("s_lnv")), ("s_lns", em.c("s_lns")),
               ("s_lnT", em.c("s_lnT")),
               ("s_ln0", em.c("s_ln0")), ("s_ln1", em.c("s_ln1"))]

    if dbg:
        dmaev("sync", dbg_x1, x1_fm[:].rearrange("p c q -> p (c q)"), "s_dbg",
              waits=list(LN_DONE))
        dmaev("sync", dbg_O, O_sb[:].rearrange("p t d -> p (t d)"), "s_dbg",
              waits=list(ATT_END))
        dmaev("sync", dbg_s, s_sb[:], "s_dbg", waits=list(ATT_END))

    stkL.close()
    stkA.close()
    stkF = ExitStack()
    hA_fl = sb("hA", [128, 32 * 1024], BF16, stkF)
    hB_fl = sb("hB", [128, 32 * 1024], BF16, stkF)
    AR = [sb("ar0", [128, 8192], BF16, stkF), sb("ar1", [128, 8192], BF16, stkF)]
    asl_sb = [sb("asl0", [128, 512], BF16, stkF), sb("asl1", [128, 512], BF16, stkF)]
    z2_sb = sb("z2_sb", [128, D], BF16, stkF)
    m2_sb = sb("m2_sb", [128, 1], F32, stkF)
    v2_sb = sb("v2_sb", [128, 1], F32, stkF)
    sd2_sb = sb("sd2_sb", [128, 1], F32, stkF)
    rstd2_sb = sb("rstd2_sb", [128, 1], F32, stkF)
    out_st = [sb("outst0", [128, D], F32, stkF)]
    def _h(t_, c, a, b):
        return t_[:, c * 1024 + a:c * 1024 + b]

    def _ar(p, base, ic, u):
        return AR[p][:, base + ic * 256 + u * 128:base + ic * 256 + (u + 1) * 128]

    rot = [0]
    _dbgw = [("s_dbg", em.c("s_dbg"))] if dbg else []
    first_ffn = {"gpsimd": list(ATT_END) + list(LN_DONE) + _dbgw,
                 "tensor": list(LN_DONE) + _dbgw, "scalar": list(LN_DONE) + _dbgw,
                 "vector": list(LN_DONE) + _dbgw}

    def takef(eng, extra=()):
        w = first_ffn.get(eng)
        if w is not None:
            first_ffn[eng] = None
            return list(extra) + w
        return list(extra)

    def fgroup(lhss, rhss, nacc, extra_waits=()):
        b = rot[0] % 4
        rot[0] += 1
        bank = FB[b]
        for ic in range(nacc):
            w = []
            if ic == 0:
                w = list(extra_waits) + [em.g(("FB", b))]
                w = takef("tensor", w)
            em.add("tensor", mm(bank[:], lhss[ic], rhss[ic],
                                ic == 0, ic == nacc - 1),
                   waits=w, inc=("s_f" if ic == nacc - 1 else None))
        return bank, em.c("s_f")

    def ffn_half(hf):
        r0 = hf * 1024
        for sl in range(16):
            o0 = sl * 256
            p = sl % 2
            g = em.g(("slab", p))
            wsem = "s_w%d" % p
            for base, wt in ((0, w1T), (2048, w2T)):
                dmaev("gpsimd",
                      AR[p][:, base:base + 2048].rearrange("p (c o) -> p c o", c=8),
                      wt[:, o0:o0 + 256].rearrange("(c p) o -> p c o", p=128),
                      wsem, waits=takef("gpsimd", [g] if g else []))
                g = None
            wdone = (wsem, em.c(wsem))
            for u in range(2):
                oc = 2 * sl + u
                for rs in range(2):
                    rsl = slice(r0 + rs * 512, r0 + (rs + 1) * 512)
                    b1, f1 = fgroup(
                        [_ar(p, 0, ic, u) for ic in range(DCH)],
                        [x1_fm[:, ic, rsl] for ic in range(DCH)], DCH,
                        extra_waits=[wdone])
                    em.add("scalar",
                           (lambda A_, B_, S_: (lambda e: e.activation(
                               A_, B_, AF.Silu, scale=S_)))(
                               asl_sb[rot[0] % 2][:], b1[:], s1_sb[:, oc:oc + 1]),
                           waits=takef("scalar",
                                       [("s_f", f1), em.g(("asl", rot[0] % 2))]),
                           inc="s_sc")
                    em.setg(("FB", (rot[0] - 1) % 4), "s_sc")
                    sc1 = em.c("s_sc")
                    b2, f2 = fgroup(
                        [_ar(p, 2048, ic, u) for ic in range(DCH)],
                        [x1_fm[:, ic, rsl] for ic in range(DCH)], DCH,
                        extra_waits=[wdone])
                    em.add("vector",
                           (lambda H_, B_, S_, A_: (lambda e: e.scalar_tensor_tensor(
                               H_, B_, S_, A_, ALU.mult, ALU.mult)))(
                               _h(hA_fl, oc, rs * 512, (rs + 1) * 512), b2[:],
                               s2_sb[:, oc:oc + 1], asl_sb[(rot[0] - 1) % 2][:]),
                           waits=takef("vector", [("s_f", f2), ("s_sc", sc1)]),
                           inc="s_ve")
                    em.setg(("FB", (rot[0] - 1) % 4), "s_ve")
                    em.setg(("asl", (rot[0] - 1) % 2), "s_ve")
            em.setg(("slab", p), "s_f")
        A_VE = em.c("s_ve")

        def stage_h(wt, src_fl, dst_fl, ssb, src_gate):
            for sl in range(16):
                o0 = sl * 256
                p = sl % 2
                g = em.g(("slab", p))
                wsem = "s_w%d" % p
                dmaev("gpsimd", AR[p][:].rearrange("p (c o) -> p c o", c=32),
                      wt[:, o0:o0 + 256].rearrange("(c p) o -> p c o", p=128),
                      wsem, waits=[g] if g else [])
                wdone = (wsem, em.c(wsem))
                for u in range(2):
                    oc = 2 * sl + u
                    for rs in range(2):
                        b, f = fgroup(
                            [_ar(p, 0, ic, u) for ic in range(32)],
                            [_h(src_fl, ic, rs * 512, (rs + 1) * 512)
                             for ic in range(32)], 32,
                            extra_waits=[wdone] + src_gate)
                        em.add("scalar",
                               (lambda A_, B_, S_: (lambda e: e.activation(
                                   A_, B_, AF.Gelu, scale=S_)))(
                                   _h(dst_fl, oc, rs * 512, (rs + 1) * 512),
                                   b[:], ssb[:, oc:oc + 1]),
                               waits=[("s_f", f)], inc="s_sc")
                        em.setg(("FB", (rot[0] - 1) % 4), "s_sc")
                em.setg(("slab", p), "s_f")
        stage_h(h1T, hA_fl, hB_fl, sh1_sb, [("s_ve", A_VE)])
        H1_SC = em.c("s_sc")
        stage_h(h2T, hB_fl, hA_fl, sh2_sb, [("s_sc", H1_SC)])
        H2_SC = em.c("s_sc")

        for g3 in range(4):
            o0 = g3 * 256
            p = g3 % 2
            gte = em.g(("slab", p))
            wsem = "s_w%d" % p
            dmaev("gpsimd", AR[p][:].rearrange("p (c o) -> p c o", c=32),
                  w3T[:, o0:o0 + 256].rearrange("(c p) o -> p c o", p=128),
                  wsem, waits=[gte] if gte else [])
            wdone = (wsem, em.c(wsem))
            for u in range(2):
                oc = 2 * g3 + u
                for rs in range(2):
                    b, f = fgroup(
                        [_ar(p, 0, ic, u) for ic in range(32)],
                        [_h(hA_fl, ic, rs * 512, (rs + 1) * 512)
                         for ic in range(32)], 32,
                        extra_waits=[wdone, ("s_sc", H2_SC)])
                    em.add("vector",
                           (lambda F_, B_, S_: (lambda e: e.tensor_scalar_mul(
                               F_, B_, S_)))(
                               _h(hB_fl, oc, rs * 512, (rs + 1) * 512),
                               b[:], s3_sb[:, oc:oc + 1]),
                           waits=[("s_f", f)], inc="s_ve")
                    em.setg(("FB", (rot[0] - 1) % 4), "s_ve")
            em.setg(("slab", p), "s_f")
        W3_VE = em.c("s_ve")
        for tt in range(8):
            t = hf * 8 + tt
            for c in range(DCH):
                w = []
                if c == 0:
                    w = [("s_ve", W3_VE), em.g(("TP",))]
                em.add("tensor",
                       (lambda O_, I_, N_: (lambda e: e.transpose(O_, I_, N_)))(
                           TP_ps[:, c * 128:(c + 1) * 128],
                           _h(hB_fl, c, tt * 128, (tt + 1) * 128), idn_sb[:]),
                       waits=w)
            for c in range(DCH):
                w = [em.g(("LT",))] if c == 0 else []
                em.add("tensor",
                       (lambda O_, I_, N_: (lambda e: e.transpose(O_, I_, N_)))(
                           LT_ps[:, c * 128:(c + 1) * 128],
                           x1_fm[:, c, r0 + tt * 128:r0 + (tt + 1) * 128],
                           idn_sb[:]),
                       waits=w, inc=("s_l2T" if c == DCH - 1 else None))
            l2t = em.c("s_l2T")
            gz = em.g(("z2r",))
            em.add("vector", cp(z2_sb[:], TP_ps[:, 0:1024]),
                   waits=[("s_l2T", l2t)] + ([gz] if gz else []))
            em.add("vector",
                   (lambda Z_, L_: (lambda e: e.tensor_add(Z_, Z_, L_)))(
                       z2_sb[:], LT_ps[:, 0:1024]),
                   inc="s_ve")
            em.setg(("TP",), "s_ve")
            em.setg(("LT",), "s_ve")
            em.add("vector",
                   (lambda Z_, M_: (lambda e: e.reduce_sum(M_, Z_, axis=AX.X)))(
                       z2_sb[:], m2_sb[:]),
                   inc="s_ve")
            zs2 = em.c("s_ve")
            em.add("scalar",
                   (lambda M_: (lambda e: e.mul(M_, M_, 1.0 / D)))(m2_sb[:]),
                   waits=[("s_ve", zs2)], inc="s_sc")
            msc = em.c("s_sc")
            em.add("scalar",
                   (lambda Q_, Z_, M_, V_: (lambda e: e.activation(
                       Q_, Z_, AF.Square, bias=M_, scale=-1.0, accum_out=V_)))(
                       hB_fl[:, 8192:9216], z2_sb[:], m2_sb[:], v2_sb[:]),
                   waits=[("s_sc", msc)])
            em.add("scalar",
                   (lambda V_, S_, E_: (lambda e: e.activation(S_, V_, AF.Sqrt,
                                                               bias=E_,
                                                               scale=1.0 / D)))(
                       v2_sb[:], sd2_sb[:], eps_sb[:]),
                   inc="s_sc")
            em.setg(("z2r",), "s_sc")
            em.add("vector",
                   (lambda S_, R_: (lambda e: e.reciprocal(R_, S_)))(
                       sd2_sb[:], rstd2_sb[:]),
                   waits=[("s_sc", em.c("s_sc"))], inc="s_vch")
            g = em.g(("outst",))
            em.add("vector",
                   (lambda Z_, M_, R_, O_: (lambda e: e.tensor_scalar(
                       O_, Z_, M_, R_, ALU.subtract, ALU.mult)))(
                       z2_sb[:], m2_sb[:], rstd2_sb[:], out_st[0][:]),
                   waits=([g] if g else []) + [("s_vch", em.c("s_vch"))], inc="s_ve")
            dmaev("scalar", out[t * 128:(t + 1) * 128, :], out_st[0][:],
                  "s_out", waits=[("s_ve", em.c("s_ve"))])
            em.setg(("outst",), "s_out")

    sems["s_lnT"] = nc.alloc_semaphore("s_lnT")

    ffn_half(0)
    ffn_half(1)

    em.add("scalar", (lambda e: e.nop()), waits=[("s_out", em.c("s_out"))])
    if dbg:
        em.add("sync", (lambda e: e.nop()), waits=[("s_dbg", em.c("s_dbg"))])

    stkF.close()
    stk.close()

    with nc.Block() as block:
        def body(engname):
            evs = em.ev[engname]
            chain = {"vector": "s_vch", "scalar": "s_sch"}.get(engname)

            def run(eng):
                seen = {}
                cnt = {}
                prev = None
                for fn, waits, inc, n in evs:
                    for s, v in waits:
                        if v > 0 and seen.get(s, 0) < v:
                            eng.wait_ge(sems[s], v)
                            seen[s] = v
                    if chain and n == 1 and prev is not None:
                        s, v = prev
                        if seen.get(s, 0) < v:
                            eng.wait_ge(sems[s], v)
                            seen[s] = v
                    ins = fn(eng)
                    use = inc if inc else (chain if (chain and n == 1) else None)
                    if use:
                        ins.then_inc(sems[use], n)
                        cnt[use] = cnt.get(use, 0) + n
                        if chain and n == 1:
                            prev = (use, cnt[use])
            return run

        block.sync(body("sync"))
        block.vector(body("vector"))
        block.scalar(body("scalar"))
        block.tensor(body("tensor"))
        block.gpsimd(body("gpsimd"))
    return nc


_BF = ml_dtypes.bfloat16


def _host_inputs(x, W, w1_w, w1_s, w2_w, w2_s, h1_w, h1_s, h2_w, h2_s, w3_w, w3_s):
    x = np.asarray(x, np.float32).reshape(SEQ, D)
    W = np.asarray(W, np.float32)
    shared = {
        "vmask": np.where((np.arange(SEQ)[None, :] // 128)
                          >= np.arange(128)[:, None], NEG, 0.0).astype(_BF),
        "tri": np.where(np.arange(128)[None, :] <= np.arange(128)[:, None],
                        0.0, NEG).astype(_BF),
        "idn": np.eye(128).astype(_BF),
        "wkT": np.ascontiguousarray(W[:D].T),
        "wvT": np.ascontiguousarray(W[D:].T),
        "w1T": np.ascontiguousarray(np.asarray(w1_w, np.float32).T),
        "w2T": np.ascontiguousarray(np.asarray(w2_w, np.float32).T),
        "h1T": np.ascontiguousarray(np.asarray(h1_w, np.float32).T),
        "h2T": np.ascontiguousarray(np.asarray(h2_w, np.float32).T),
        "w3T": np.ascontiguousarray(np.asarray(w3_w, np.float32).T),
        "s1i": np.ascontiguousarray(np.asarray(w1_s, np.float32).reshape(32, 128).T),
        "s2i": np.ascontiguousarray(np.asarray(w2_s, np.float32).reshape(32, 128).T),
        "sh1i": np.ascontiguousarray(np.asarray(h1_s, np.float32).reshape(32, 128).T),
        "sh2i": np.ascontiguousarray(np.asarray(h2_s, np.float32).reshape(32, 128).T),
        "s3i": np.ascontiguousarray(np.asarray(w3_s, np.float32).reshape(8, 128).T),
        "epsi": np.full((128, 1), 1e-5, np.float32),
    }
    maps = []
    for c in range(NCORES):
        tiles = [8 * s + c for s in range(NT)]
        rows = np.concatenate([np.arange(g * 128, (g + 1) * 128) for g in tiles])
        xq_h = np.ascontiguousarray(x[rows])
        u = np.zeros((128, 2048), np.float32)
        for s, g in enumerate(tiles):
            u[g, s * 128:(s + 1) * 128] = 1.0
        maps.append(dict(shared,
                         xqT=np.ascontiguousarray(xq_h.T),
                         umask=u.astype(_BF)))
    return maps


def kernel(L0_attn_x, L0_attn_W, L0_ffn_w1_w, L0_ffn_w1_s, L0_ffn_w2_w,
           L0_ffn_w2_s, L0_ffn_h1_w, L0_ffn_h1_s, L0_ffn_h2_w, L0_ffn_h2_s,
           L0_ffn_w3_w, L0_ffn_w3_s):
    maps = _host_inputs(L0_attn_x, L0_attn_W, L0_ffn_w1_w, L0_ffn_w1_s,
                        L0_ffn_w2_w, L0_ffn_w2_s, L0_ffn_h1_w, L0_ffn_h1_s,
                        L0_ffn_h2_w, L0_ffn_h2_s, L0_ffn_w3_w, L0_ffn_w3_s)
    nc = build_nc()
    res = run_bass_kernel_spmd(nc, maps, core_ids=list(range(NCORES)))
    full = np.empty((SEQ, D), np.float32)
    for c in range(NCORES):
        o = res.results[c]["out"]
        for s in range(NT):
            g = 8 * s + c
            full[g * 128:(g + 1) * 128] = o[s * 128:(s + 1) * 128]
    return full.reshape(1, SEQ, D)


# version 07
# speedup vs baseline: 1.2771x; 1.2771x over previous
import numpy as np
import ml_dtypes
from contextlib import ExitStack

import concourse.bass as bass
import concourse.mybir as mybir
from concourse.bass_utils import run_bass_kernel_spmd

F32 = mybir.dt.float32
BF16 = mybir.dt.bfloat16
AF = mybir.ActivationFunctionType
ALU = mybir.AluOpType
AX = mybir.AxisListType

SEQ = 16384
D = 1024
NCORES = 8
NT = 16
NBLK = 32
BK = 512
DCH = 8
DH = 4096
NEG = -30000.0

ENGS = ("sync", "vector", "scalar", "tensor", "gpsimd")


class _Em:

    def __init__(self):
        self.ev = {e: [] for e in ENGS}
        self.cnt = {}
        self.gate = {}

    def c(self, s):
        return self.cnt.get(s, 0)

    def add(self, eng, fn, waits=(), inc=None, dma=False):
        n = 16 if dma else 1
        if inc:
            self.cnt[inc] = self.c(inc) + n
        self.ev[eng].append((fn, tuple(w for w in waits if w is not None), inc, n))
        return self.c(inc) if inc else None

    def g(self, key):
        return self.gate.get(key)

    def setg(self, key, sem):
        self.gate[key] = (sem, self.c(sem))


def _parts(j):
    return [t for t in range(NT) if j < 2 * t + 2]


def build_nc(dbg=False, nblk=NBLK, serial=False):
    nc = bass.Bass("TRN2", target_bir_lowering=False, debug=False,
                   detect_race_conditions=False)
    dram = nc.dram_tensor

    xqT = dram("xqT", [D, 2048], F32, kind="ExternalInput").ap()
    cc_src = [dram("ccs%d" % s, [262144], BF16, kind="Internal").ap()
              for s in range(NT)]
    cc_dst = [dram("ccd%d" % s, [8, 262144], BF16, kind="Internal").ap()
              for s in range(NT)]
    umask = dram("umask", [128, 2048], BF16, kind="ExternalInput").ap()
    vmask = dram("vmask", [128, SEQ], BF16, kind="ExternalInput").ap()
    tri = dram("tri", [128, 128], BF16, kind="ExternalInput").ap()
    idn = dram("idn", [128, 128], BF16, kind="ExternalInput").ap()
    wkT = dram("wkT", [D, D], F32, kind="ExternalInput").ap()
    wvT = dram("wvT", [D, D], F32, kind="ExternalInput").ap()
    w1T = dram("w1T", [D, DH], F32, kind="ExternalInput").ap()
    w2T = dram("w2T", [D, DH], F32, kind="ExternalInput").ap()
    h1T = dram("h1T", [DH, DH], F32, kind="ExternalInput").ap()
    h2T = dram("h2T", [DH, DH], F32, kind="ExternalInput").ap()
    w3T = dram("w3T", [DH, D], F32, kind="ExternalInput").ap()
    s1i = dram("s1i", [128, 32], F32, kind="ExternalInput").ap()
    s2i = dram("s2i", [128, 32], F32, kind="ExternalInput").ap()
    sh1i = dram("sh1i", [128, 32], F32, kind="ExternalInput").ap()
    sh2i = dram("sh2i", [128, 32], F32, kind="ExternalInput").ap()
    s3i = dram("s3i", [128, 8], F32, kind="ExternalInput").ap()
    epsi = dram("epsi", [128, 1], F32, kind="ExternalInput").ap()
    out = dram("out", [2048, D], F32, kind="ExternalOutput").ap()
    if dbg:
        dbg_x1 = dram("dbg_x1", [128, DCH * 2048], BF16, kind="ExternalOutput").ap()
        dbg_O = dram("dbg_O", [128, NT * 1024], F32, kind="ExternalOutput").ap()
        dbg_s = dram("dbg_s", [128, NT], F32, kind="ExternalOutput").ap()

    sems = {}
    for s in ("s_in", "s_inw", "s_xk0", "s_xk1", "s_vm0", "s_vm1",
              "s_kvg", "s_kve", "s_S",
              "s_vstat", "s_exp", "s_T", "s_PTc", "s_PV", "s_Oupd",
              "s_ln0", "s_ln1", "s_lnv", "s_lns", "s_cc", "s_pk0", "s_pk1",
              "s_kv0", "s_kv1",
              "s_w0", "s_w1", "s_f", "s_sc", "s_ve", "s_l2T", "s_out", "s_dbg",
              "s_vch", "s_sch"):
        sems[s] = nc.alloc_semaphore(s)

    pa = nc.alloc_psum_tensor
    S_ps = [pa("S0", [128, 512], F32), pa("S1", [128, 512], F32)]
    KV_ps = [pa("KVA", [128, 512], F32), pa("KVB", [128, 512], F32)]
    PV_ps = pa("PVp", [128, 1024], F32)
    TP_ps = pa("TPp", [128, 1024], BF16)
    LT_ps = pa("LTp", [128, 1024], BF16)
    FB = [S_ps[0], S_ps[1], KV_ps[0], KV_ps[1]]

    em = _Em()

    def mm(psl, lhsT, rhs, start, stop):
        return (lambda P_, L_, R_, st_, sp_:
                (lambda e: e.matmul(P_, L_, R_, start=st_, stop=sp_)))(
                    psl, lhsT, rhs, start, stop)

    def cp(dst, src):
        return (lambda D_, S_: (lambda e: e.tensor_copy(D_, S_)))(dst, src)

    def dmaev(eng, dst, src, inc, waits=()):
        em.add(eng, (lambda d, s: (lambda e: e.dma_start(out=d, in_=s)))(dst, src),
               waits=waits, inc=inc, dma=True)

    stk = ExitStack()

    def sb(name, shape, dt, st=stk, side=None):
        kw = {} if side is None else {"side": side}
        return st.enter_context(nc.sbuf_tensor(name, shape, dt, **kw))

    idn_sb = sb("idn_sb", [128, 128], BF16)
    s1_sb = sb("s1_sb", [128, 32], F32)
    s2_sb = sb("s2_sb", [128, 32], F32)
    sh1_sb = sb("sh1_sb", [128, 32], F32)
    sh2_sb = sb("sh2_sb", [128, 32], F32)
    s3_sb = sb("s3_sb", [128, 8], F32)
    eps_sb = sb("eps_sb", [128, 1], F32)

    stkA = ExitStack()
    O_sb = sb("O_sb", [128, NT, 1024], F32, stkA)
    x_fm = sb("x_fm", [128, DCH, 2048], BF16, stkA)
    s_sb = sb("s_sb", [128, NT], F32, stkA)
    nm_sb = sb("nm_sb", [128, NT], F32, stkA)
    nmn_sb = sb("nmn_sb", [128, 2], F32, stkA)
    d_sb = sb("d_sb", [128, 2], F32, stkA)
    ct_sb = sb("ct_sb", [128, 2], F32, stkA)
    sbk_sb = sb("sbk_sb", [128, 2], F32, stkA)
    bmax_sb = sb("bmax_sb", [128, 2], F32, stkA)

    stkB = ExitStack()
    u_sb = sb("u_sb", [128, 2048], BF16, stkB)
    wk_sb = sb("wk_sb", [128, DCH, 1024], BF16, stkB)
    wv_sb = sb("wv_sb", [128, DCH, 1024], BF16, stkB)
    K_sb = [sb("K0", [128, DCH, BK], BF16, stkB), sb("K1", [128, DCH, BK], BF16, stkB)]
    V_sb = [sb("V0", [128, 4, 1024], BF16, stkB), sb("V1", [128, 4, 1024], BF16, stkB)]
    vm_sb = [sb("vm0", [128, BK], BF16, stkB), sb("vm1", [128, BK], BF16, stkB)]
    Kt_sb = [sb("Kt0", [128, DCH, 128], BF16, stkB), sb("Kt1", [128, DCH, 128], BF16, stkB)]
    Vt_sb = [sb("Vt0", [128, 1024], BF16, stkB), sb("Vt1", [128, 1024], BF16, stkB)]
    P_sb = [sb("P0", [128, 512], BF16, stkB), sb("P1", [128, 512], BF16, stkB)]
    PT_sb = [sb("PT0", [128, 512], BF16, stkB), sb("PT1", [128, 512], BF16, stkB)]
    tri_sb = sb("tri_sb", [128, 128], BF16, stkB)

    dmaev("sync", idn_sb[:], idn, "s_in")
    dmaev("sync", tri_sb[:], tri, "s_in")
    dmaev("sync", u_sb[:], umask, "s_in")
    dmaev("sync", s1_sb[:], s1i, "s_in")
    dmaev("sync", s2_sb[:], s2i, "s_in")
    dmaev("sync", sh1_sb[:], sh1i, "s_in")
    dmaev("sync", sh2_sb[:], sh2i, "s_in")
    dmaev("sync", s3_sb[:], s3i, "s_in")
    dmaev("sync", eps_sb[:], epsi, "s_in")
    SETUP_SYNC = em.c("s_in")

    dmaev("gpsimd", x_fm[:], xqT.rearrange("(c p) q -> p c q", p=128), "s_inw")
    dmaev("gpsimd", wk_sb[:], wkT.rearrange("(c p) o -> p c o", p=128), "s_inw")
    dmaev("gpsimd", wv_sb[:], wvT.rearrange("(c p) o -> p c o", p=128), "s_inw")
    SETUP_GP = em.c("s_inw")

    VM_LOADED = {}
    KVE_BLK = {}

    def load_vm(j):
        g = em.g(("vm", j % 2))
        sname = "s_vm%d" % (j % 2)
        dmaev("gpsimd", vm_sb[j % 2][:], vmask[:, j * BK:(j + 1) * BK],
              sname, waits=([g] if g else []))
        VM_LOADED[j] = (sname, em.c(sname))

    def load_kv(j):
        s, cb = j // 2, 4 * (j % 2)
        pr = j % 2
        sname = "s_kv%d" % pr
        gk = em.g(("Kpar", pr))
        gv = em.g(("Vpar", pr))
        w = [("s_cc", s + 1)] + [x for x in (gk, gv) if x]
        dmaev("gpsimd",
              K_sb[pr][:].rearrange("p c k -> p c k"),
              cc_dst[s][cb:cb + 4, 0:131072].rearrange(
                  "t (p c k) -> p c t k", p=128, c=DCH),
              sname, waits=w)
        dmaev("gpsimd",
              V_sb[pr][:],
              cc_dst[s][cb:cb + 4, 131072:262144].rearrange(
                  "t (p v) -> p t v", p=128),
              sname)
        KVE_BLK[j] = (sname, em.c(sname))

    load_vm(0)
    load_vm(1)

    rg8 = [list(range(8))]
    for t in range(NT):
        pr = t % 2
        for g in range(10):
            bank = KV_ps[g % 2]
            if g < 8:
                c8 = g
                lhss = [wk_sb[:, ic, c8 * 128:(c8 + 1) * 128] for ic in range(DCH)]
                rhss = [x_fm[:, ic, t * 128:(t + 1) * 128] for ic in range(DCH)]
                po = bank[:, 0:128]
                dst = Kt_sb[pr][:, c8, :]
                gates = [em.g(("Ktpar", pr)), em.g(("Ktpk", pr))]
            else:
                h = g - 8
                lhss = [x_fm[:, ic, t * 128:(t + 1) * 128] for ic in range(DCH)]
                rhss = [wv_sb[:, ic, h * 512:(h + 1) * 512] for ic in range(DCH)]
                po = bank[:]
                dst = Vt_sb[pr][:, h * 512:(h + 1) * 512]
                gates = [em.g(("Vtpar", pr)), em.g(("Vtpk", pr))]
            for ic in range(DCH):
                w = []
                if ic == 0:
                    w = [("s_inw", SETUP_GP), em.g(("KVps", g % 2))]
                em.add("tensor", mm(po, lhss[ic], rhss[ic], ic == 0, ic == DCH - 1),
                       waits=w, inc=("s_kvg" if ic == DCH - 1 else None))
            w = [("s_kvg", em.c("s_kvg"))] + gates
            em.add("vector", cp(dst, po), waits=w, inc="s_kve")
            em.setg(("KVps", g % 2), "s_kve")
        KV_T = em.c("s_kve")
        pks = "s_pk%d" % pr
        dmaev("gpsimd", cc_src[t][0:131072].rearrange("(p c k) -> p c k",
                                                      p=128, c=DCH),
              Kt_sb[pr][:], pks, waits=[("s_kve", KV_T)])
        dmaev("gpsimd", cc_src[t][131072:262144].rearrange("(p v) -> p v", p=128),
              Vt_sb[pr][:], pks)
        pkv = em.c(pks)
        em.add("gpsimd",
               (lambda S_, D_: (lambda e: e.collective_compute(
                   "AllGather", ALU.bypass, rg8, ins=[S_], outs=[D_])))(
                   cc_src[t], cc_dst[t]),
               waits=[(pks, pkv)], inc="s_cc")
        em.setg(("Ktpk", pr), pks)
        em.setg(("Vtpk", pr), pks)
        for ic in range(DCH + 1):
            w = []
            if ic == 0:
                w = [("s_kve", KV_T), em.g(("Spar", pr)), ("s_in", SETUP_SYNC)]
            if ic < DCH:
                lhsT = x_fm[:, ic, t * 128:(t + 1) * 128]
                rhs = Kt_sb[pr][:, ic, :]
            else:
                lhsT = idn_sb[:]
                rhs = tri_sb[:]
            em.add("tensor", mm(S_ps[pr][:, 0:128], lhsT, rhs,
                                ic == 0, ic == DCH),
                   waits=w, inc=("s_S" if ic == DCH else None))
        em.setg(("Ktpar", pr), "s_S")
        sS = em.c("s_S")
        em.add("vector",
               (lambda P_, M_: (lambda e: e.reduce_max(M_, P_, axis=AX.X,
                                                       negate=True)))(
                   S_ps[pr][:, 0:128], nm_sb[:, t:t + 1]),
               waits=[("s_S", sS)], inc="s_vstat")
        vst = em.c("s_vstat")
        em.add("scalar",
               (lambda Pp, Sp, B_, A_: (lambda e: e.activation(
                   Pp, Sp, AF.Exp, bias=B_, accum_out=A_)))(
                   P_sb[pr][:, 0:128], S_ps[pr][:, 0:128], nm_sb[:, t:t + 1],
                   s_sb[:, t:t + 1]),
               waits=[("s_vstat", vst), em.g(("Ppar", pr))], inc="s_exp")
        em.setg(("Spar", pr), "s_exp")
        sE = em.c("s_exp")
        em.add("tensor",
               (lambda O_, I_, N_: (lambda e: e.transpose(O_, I_, N_)))(
                   TP_ps[:, 0:128], P_sb[pr][:, 0:128], idn_sb[:]),
               waits=[("s_exp", sE), em.g(("TP",)), ("s_in", SETUP_SYNC)],
               inc="s_T")
        em.setg(("Ppar", pr), "s_T")
        em.add("vector", cp(PT_sb[pr][:, 0:128], TP_ps[:, 0:128]),
               waits=[("s_T", em.c("s_T")), em.g(("PTpar", pr))], inc="s_PTc")
        em.setg(("TP",), "s_PTc")
        ptc = em.c("s_PTc")
        for h in range(2):
            em.add("tensor",
                   mm(PV_ps[:, h * 512:(h + 1) * 512], PT_sb[pr][:, 0:128],
                      Vt_sb[pr][:, h * 512:(h + 1) * 512], True, True),
                   waits=([("s_PTc", ptc), em.g(("PVps",))] if h == 0 else ()),
                   inc=("s_PV" if h == 1 else None))
        em.setg(("PTpar", pr), "s_PV")
        em.setg(("Vtpar", pr), "s_PV")
        spv = em.c("s_PV")
        for h in range(2):
            em.add("vector", cp(O_sb[:, t, h * 512:(h + 1) * 512],
                                PV_ps[:, h * 512:(h + 1) * 512]),
                   waits=([("s_PV", spv)] if h == 0 else ()),
                   inc=("s_Oupd" if h == 1 else None))
        em.setg(("PVps",), "s_Oupd")

    def emit_ln1(t, owait, pr):
        zO = zo_sb[pr]
        for c in range(DCH):
            w = [em.g(("LT",))] if c == 0 else []
            em.add("tensor",
                   (lambda O_, I_, N_: (lambda e: e.transpose(O_, I_, N_)))(
                       LT_ps[:, c * 128:(c + 1) * 128],
                       x_fm[:, c, t * 128:(t + 1) * 128], idn_sb[:]),
                   waits=w, inc=("s_lnT" if c == DCH - 1 else None))
        xT_done = em.c("s_lnT")
        em.add("vector",
               (lambda S_, R_: (lambda e: e.reciprocal(R_, S_)))(
                   s_sb[:, t:t + 1], rs_sb[pr][:]),
               waits=owait, inc="s_vch")
        em.add("vector",
               (lambda O_, R_, Z_: (lambda e: e.tensor_scalar_mul(Z_, O_, R_)))(
                   O_sb[:, t, :], rs_sb[pr][:], zO[:]),
               waits=[("s_vch", em.c("s_vch"))])
        em.add("vector",
               (lambda L_, Z_, ZB: (lambda e: e.scalar_tensor_tensor(
                   ZB, L_, 2.0, Z_, ALU.mult, ALU.add)))(
                   LT_ps[:, 0:1024], zO[:], z_sb[pr][:]),
               waits=[("s_lnT", xT_done)], inc="s_lnv")
        em.setg(("LT",), "s_lnv")
        em.add("vector",
               (lambda Z_, M_: (lambda e: e.reduce_sum(M_, Z_, axis=AX.X)))(
                   z_sb[pr][:], mu_sb[pr][:]),
               inc="s_lnv")
        zs = em.c("s_lnv")
        em.add("scalar",
               (lambda M_: (lambda e: e.mul(M_, M_, 1.0 / D)))(mu_sb[pr][:]),
               waits=[("s_lnv", zs)], inc="s_lns")
        em.add("scalar",
               (lambda Q_, Z_, M_, V_: (lambda e: e.activation(
                   Q_, Z_, AF.Square, bias=M_, scale=-1.0, accum_out=V_)))(
                   zo_sb[pr][:], z_sb[pr][:], mu_sb[pr][:], vs_sb[pr][:]))
        em.add("scalar",
               (lambda V_, S_, E_: (lambda e: e.activation(S_, V_, AF.Sqrt,
                                                           bias=E_, scale=1.0 / D)))(
                   vs_sb[pr][:], sd_sb[pr][:], eps_sb[:]),
               waits=[("s_in", SETUP_SYNC)], inc="s_lns")
        em.add("vector",
               (lambda S_, R_: (lambda e: e.reciprocal(R_, S_)))(
                   sd_sb[pr][:], rstd_sb[pr][:]),
               waits=[("s_lns", em.c("s_lns"))], inc="s_vch")
        em.add("vector",
               (lambda Z_, M_, R_, X_: (lambda e: e.tensor_scalar(
                   X_, Z_, M_, R_, ALU.subtract, ALU.mult)))(
                   z_sb[pr][:], mu_sb[pr][:], rstd_sb[pr][:], x1b_sb[pr][:]),
               waits=[("s_vch", em.c("s_vch"))], inc="s_lnv")
        x1done = em.c("s_lnv")
        for c in range(DCH):
            w = [em.g(("TP",)), ("s_lnv", x1done)] if c == 0 else []
            em.add("tensor",
                   (lambda O_, I_, N_: (lambda e: e.transpose(O_, I_, N_)))(
                       TP_ps[:, c * 128:(c + 1) * 128],
                       x1b_sb[pr][:, c * 128:(c + 1) * 128], idn_sb[:]),
                   waits=w, inc=("s_lnT" if c == DCH - 1 else None))
        for half in range(2):
            em.add("vector",
                   (lambda D2_, S2_: (lambda e: e.tensor_copy(D2_, S2_)))(
                       x1_fm[:, half * 4:(half + 1) * 4, t * 128:(t + 1) * 128],
                       TP_ps[:, half * 512:(half + 1) * 512].rearrange(
                           "p (c q) -> p c q", c=4)),
                   waits=([("s_lnT", em.c("s_lnT"))] if half == 0 else ()),
                   inc=("s_lnv" if half == 1 else None))
        em.setg(("TP",), "s_lnv")

    jobq = [(j, t) for j in range(nblk) for t in _parts(j)]
    NJOBS = len(jobq)
    n_start = {}
    n_end = {}
    idx = 0
    for j in range(nblk):
        n_start[j] = idx
        idx += len(_parts(j))
        n_end[j] = idx
    S_OF, VST_OF, EXP_OF, T_OF, PTC_OF, PV_OF, OUP_OF = {}, {}, {}, {}, {}, {}, {}

    if nblk > 0:
        load_kv(0)
    if nblk > 1:
        load_kv(1)

    def emit_S(n):
        j, t = jobq[n]
        pr = j % 2
        bank = S_ps[n % 2]
        for ic in range(DCH + 1):
            if ic == 0:
                w = [KVE_BLK[j], em.g(("Spar", n % 2))]
                if serial:
                    w += [(s_, em.c(s_)) for s_ in
                          ("s_Oupd", "s_exp", "s_PTc", "s_kve", "s_vstat")]
            elif ic == DCH:
                w = [VM_LOADED[j], ("s_in", SETUP_SYNC)]
            else:
                w = []
            if ic < DCH:
                lhsT = x_fm[:, ic, t * 128:(t + 1) * 128]
                rhs = K_sb[pr][:, ic, :]
            else:
                lhsT = u_sb[:, t * 128:(t + 1) * 128]
                rhs = vm_sb[pr][:]
            em.add("tensor", mm(bank[:], lhsT, rhs, ic == 0, ic == DCH),
                   waits=w, inc=("s_S" if ic == DCH else None))
        S_OF[n] = em.c("s_S")
        if n + 1 == n_end[j]:
            em.setg(("Kpar", pr), "s_S")
            em.setg(("vm", pr), "s_S")

    def emit_stats(n):
        j, t = jobq[n]
        q = n % 2
        bank = S_ps[q]
        em.add("vector",
               (lambda B_, M_: (lambda e: e.reduce_max(M_, B_, axis=AX.X,
                                                       negate=True)))(
                   bank[:], bmax_sb[:, q:q + 1]),
               waits=[("s_S", S_OF[n])], inc="s_vch")
        em.add("vector",
               (lambda B_, NM_, D_: (lambda e: e.tensor_scalar(
                   D_, B_, NM_, 0.0, ALU.subtract, ALU.min)))(
                   bmax_sb[:, q:q + 1], nm_sb[:, t:t + 1], d_sb[:, q:q + 1]),
               waits=[("s_vch", em.c("s_vch"))])
        em.add("vector",
               (lambda NM_, D_: (lambda e: e.tensor_scalar_add(NM_, NM_, D_)))(
                   nm_sb[:, t:t + 1], d_sb[:, q:q + 1]),
               inc="s_vstat")
        VST_OF[n] = em.c("s_vstat")
        em.add("scalar",
               (lambda D_, C_: (lambda e: e.activation(C_, D_, AF.Exp)))(
                   d_sb[:, q:q + 1], ct_sb[:, q:q + 1]),
               waits=[("s_vstat", VST_OF[n])])
        em.add("scalar",
               (lambda Pp, Sp, B_, A_: (lambda e: e.activation(
                   Pp, Sp, AF.Exp, bias=B_, accum_out=A_)))(
                   P_sb[q][:], bank[:], nm_sb[:, t:t + 1], sbk_sb[:, q:q + 1]),
               waits=[em.g(("Ppar", q))], inc="s_exp")
        em.setg(("Spar", q), "s_exp")
        EXP_OF[n] = em.c("s_exp")

    def emit_supd(n):
        j, t = jobq[n]
        q = n % 2
        em.add("vector",
               (lambda S_, C_, B_: (lambda e: e.scalar_tensor_tensor(
                   S_, S_, C_, B_, ALU.mult, ALU.add)))(
                   s_sb[:, t:t + 1], ct_sb[:, q:q + 1], sbk_sb[:, q:q + 1]),
               waits=[("s_exp", EXP_OF[n])])

    def emit_T(n):
        q = n % 2
        for kc in range(4):
            w = []
            if kc == 0:
                w = [("s_exp", EXP_OF[n]), em.g(("TP",))]
            em.add("tensor",
                   (lambda O_, I_, N_: (lambda e: e.transpose(O_, I_, N_)))(
                       TP_ps[:, kc * 128:(kc + 1) * 128],
                       P_sb[q][:, kc * 128:(kc + 1) * 128], idn_sb[:]),
                   waits=w, inc=("s_T" if kc == 3 else None))
        em.setg(("Ppar", q), "s_T")
        T_OF[n] = em.c("s_T")
        em.add("scalar",
               (lambda S_, D_: (lambda e: e.copy(D_, S_)))(
                   TP_ps[:, 0:512], PT_sb[q][:]),
               waits=[("s_T", T_OF[n]), em.g(("PTpar", q))], inc="s_PTc")
        em.setg(("TP",), "s_PTc")
        PTC_OF[n] = em.c("s_PTc")

    def emit_PV(n):
        j, t = jobq[n]
        pr = j % 2
        q = n % 2
        for h in range(2):
            for kc in range(4):
                w = []
                if h == 0 and kc == 0:
                    w = [("s_PTc", PTC_OF[n]), em.g(("PVps",))]
                em.add("tensor",
                       mm(PV_ps[:, h * 512:(h + 1) * 512],
                          PT_sb[q][:, kc * 128:(kc + 1) * 128],
                          V_sb[pr][:, kc, h * 512:(h + 1) * 512],
                          kc == 0, kc == 3),
                       waits=w, inc=("s_PV" if (h == 1 and kc == 3) else None))
        em.setg(("PTpar", q), "s_PV")
        if n + 1 == n_end[j]:
            em.setg(("Vpar", pr), "s_PV")
        PV_OF[n] = em.c("s_PV")
        for h in range(2):
            em.add("vector",
                   (lambda O_, C_, Pp: (lambda e: e.scalar_tensor_tensor(
                       O_, O_, C_, Pp, ALU.mult, ALU.add)))(
                       O_sb[:, t, h * 512:(h + 1) * 512], ct_sb[:, q:q + 1],
                       PV_ps[:, h * 512:(h + 1) * 512]),
                   waits=([("s_PV", PV_OF[n])] if h == 0 else ()),
                   inc=("s_Oupd" if h == 1 else None))
        em.setg(("PVps",), "s_Oupd")
        OUP_OF[n] = em.c("s_Oupd")

    emitted = [0]

    def try_emit_S():
        if emitted[0] < NJOBS:
            jn, _ = jobq[emitted[0]]
            if jn in KVE_BLK:
                emit_S(emitted[0])
                emitted[0] += 1

    try_emit_S()
    emit_stats(0)
    for n in range(NJOBS):
        j, t = jobq[n]
        try_emit_S()
        emit_T(n)
        if n + 1 < NJOBS:
            emit_stats(n + 1)
        emit_PV(n)
        emit_supd(n)
        if n + 1 == n_end[j] and j + 2 < nblk:
            load_kv(j + 2)
            load_vm(j + 2)

    ATT_END = [("s_S", em.c("s_S")), ("s_PV", em.c("s_PV")),
               ("s_Oupd", em.c("s_Oupd")), ("s_exp", em.c("s_exp")),
               ("s_PTc", em.c("s_PTc")), ("s_kve", em.c("s_kve")),
               ("s_vstat", em.c("s_vstat"))]


    stkB.close()
    x1_fm = sb("x1_fm", [128, DCH, 2048], BF16, side="right")
    stkL = ExitStack()
    z_sb = [sb("z0", [128, D], BF16, stkL), sb("z1", [128, D], BF16, stkL)]
    zo_sb = [sb("zo0", [128, D], F32, stkL), sb("zo1", [128, D], F32, stkL)]
    x1b_sb = [sb("x1b0", [128, D], BF16, stkL), sb("x1b1", [128, D], BF16, stkL)]
    rs_sb = [sb("rs0", [128, 1], F32, stkL), sb("rs1", [128, 1], F32, stkL)]
    mu_sb = [sb("mu0", [128, 1], F32, stkL), sb("mu1", [128, 1], F32, stkL)]
    vs_sb = [sb("vs0", [128, 1], F32, stkL), sb("vs1", [128, 1], F32, stkL)]
    sd_sb = [sb("sd0", [128, 1], F32, stkL), sb("sd1", [128, 1], F32, stkL)]
    rstd_sb = [sb("rstd0", [128, 1], F32, stkL), sb("rstd1", [128, 1], F32, stkL)]
    for t in range(NT):
        emit_ln1(t, list(ATT_END) if t == 0 else [], t % 2)
    LN_DONE = [("s_lnv", em.c("s_lnv")), ("s_lns", em.c("s_lns")),
               ("s_lnT", em.c("s_lnT")),
               ("s_ln0", em.c("s_ln0")), ("s_ln1", em.c("s_ln1"))]

    if dbg:
        dmaev("sync", dbg_x1, x1_fm[:].rearrange("p c q -> p (c q)"), "s_dbg",
              waits=list(LN_DONE))
        dmaev("sync", dbg_O, O_sb[:].rearrange("p t d -> p (t d)"), "s_dbg",
              waits=list(ATT_END))
        dmaev("sync", dbg_s, s_sb[:], "s_dbg", waits=list(ATT_END))

    stkL.close()
    stkA.close()
    stkF = ExitStack()
    hA_fl = sb("hA", [128, 32 * 1024], BF16, stkF)
    hB_fl = sb("hB", [128, 32 * 1024], BF16, stkF)
    AR = [sb("ar0", [128, 8192], BF16, stkF), sb("ar1", [128, 8192], BF16, stkF)]
    asl_sb = [sb("asl0", [128, 512], BF16, stkF), sb("asl1", [128, 512], BF16, stkF)]
    z2_sb = sb("z2_sb", [128, D], BF16, stkF)
    m2_sb = sb("m2_sb", [128, 1], F32, stkF)
    v2_sb = sb("v2_sb", [128, 1], F32, stkF)
    sd2_sb = sb("sd2_sb", [128, 1], F32, stkF)
    rstd2_sb = sb("rstd2_sb", [128, 1], F32, stkF)
    out_st = [sb("outst0", [128, D], F32, stkF)]
    def _h(t_, c, a, b):
        return t_[:, c * 1024 + a:c * 1024 + b]

    def _ar(p, base, ic, u):
        return AR[p][:, base + ic * 256 + u * 128:base + ic * 256 + (u + 1) * 128]

    rot = [0]
    _dbgw = [("s_dbg", em.c("s_dbg"))] if dbg else []
    first_ffn = {"gpsimd": list(ATT_END) + list(LN_DONE) + _dbgw,
                 "tensor": list(LN_DONE) + _dbgw, "scalar": list(LN_DONE) + _dbgw,
                 "vector": list(LN_DONE) + _dbgw}

    def takef(eng, extra=()):
        w = first_ffn.get(eng)
        if w is not None:
            first_ffn[eng] = None
            return list(extra) + w
        return list(extra)

    def fgroup(lhss, rhss, nacc, extra_waits=()):
        b = rot[0] % 4
        rot[0] += 1
        bank = FB[b]
        for ic in range(nacc):
            w = []
            if ic == 0:
                w = list(extra_waits) + [em.g(("FB", b))]
                w = takef("tensor", w)
            em.add("tensor", mm(bank[:], lhss[ic], rhss[ic],
                                ic == 0, ic == nacc - 1),
                   waits=w, inc=("s_f" if ic == nacc - 1 else None))
        return bank, em.c("s_f")

    def ffn_half(hf):
        r0 = hf * 1024
        for sl in range(16):
            o0 = sl * 256
            p = sl % 2
            g = em.g(("slab", p))
            wsem = "s_w%d" % p
            for base, wt in ((0, w1T), (2048, w2T)):
                dmaev("gpsimd",
                      AR[p][:, base:base + 2048].rearrange("p (c o) -> p c o", c=8),
                      wt[:, o0:o0 + 256].rearrange("(c p) o -> p c o", p=128),
                      wsem, waits=takef("gpsimd", [g] if g else []))
                g = None
            wdone = (wsem, em.c(wsem))
            for u in range(2):
                oc = 2 * sl + u
                for rs in range(2):
                    rsl = slice(r0 + rs * 512, r0 + (rs + 1) * 512)
                    b1, f1 = fgroup(
                        [_ar(p, 0, ic, u) for ic in range(DCH)],
                        [x1_fm[:, ic, rsl] for ic in range(DCH)], DCH,
                        extra_waits=[wdone])
                    em.add("scalar",
                           (lambda A_, B_, S_: (lambda e: e.activation(
                               A_, B_, AF.Silu, scale=S_)))(
                               asl_sb[rot[0] % 2][:], b1[:], s1_sb[:, oc:oc + 1]),
                           waits=takef("scalar",
                                       [("s_f", f1), em.g(("asl", rot[0] % 2))]),
                           inc="s_sc")
                    em.setg(("FB", (rot[0] - 1) % 4), "s_sc")
                    sc1 = em.c("s_sc")
                    b2, f2 = fgroup(
                        [_ar(p, 2048, ic, u) for ic in range(DCH)],
                        [x1_fm[:, ic, rsl] for ic in range(DCH)], DCH,
                        extra_waits=[wdone])
                    em.add("vector",
                           (lambda H_, B_, S_, A_: (lambda e: e.scalar_tensor_tensor(
                               H_, B_, S_, A_, ALU.mult, ALU.mult)))(
                               _h(hA_fl, oc, rs * 512, (rs + 1) * 512), b2[:],
                               s2_sb[:, oc:oc + 1], asl_sb[(rot[0] - 1) % 2][:]),
                           waits=takef("vector", [("s_f", f2), ("s_sc", sc1)]),
                           inc="s_ve")
                    em.setg(("FB", (rot[0] - 1) % 4), "s_ve")
                    em.setg(("asl", (rot[0] - 1) % 2), "s_ve")
            em.setg(("slab", p), "s_f")
        A_VE = em.c("s_ve")

        def stage_h(wt, src_fl, dst_fl, ssb, src_gate):
            for sl in range(16):
                o0 = sl * 256
                p = sl % 2
                g = em.g(("slab", p))
                wsem = "s_w%d" % p
                dmaev("gpsimd", AR[p][:].rearrange("p (c o) -> p c o", c=32),
                      wt[:, o0:o0 + 256].rearrange("(c p) o -> p c o", p=128),
                      wsem, waits=[g] if g else [])
                wdone = (wsem, em.c(wsem))
                for u in range(2):
                    oc = 2 * sl + u
                    for rs in range(2):
                        b, f = fgroup(
                            [_ar(p, 0, ic, u) for ic in range(32)],
                            [_h(src_fl, ic, rs * 512, (rs + 1) * 512)
                             for ic in range(32)], 32,
                            extra_waits=[wdone] + src_gate)
                        em.add("scalar",
                               (lambda A_, B_, S_: (lambda e: e.activation(
                                   A_, B_, AF.Gelu, scale=S_)))(
                                   _h(dst_fl, oc, rs * 512, (rs + 1) * 512),
                                   b[:], ssb[:, oc:oc + 1]),
                               waits=[("s_f", f)], inc="s_sc")
                        em.setg(("FB", (rot[0] - 1) % 4), "s_sc")
                em.setg(("slab", p), "s_f")
        stage_h(h1T, hA_fl, hB_fl, sh1_sb, [("s_ve", A_VE)])
        H1_SC = em.c("s_sc")
        stage_h(h2T, hB_fl, hA_fl, sh2_sb, [("s_sc", H1_SC)])
        H2_SC = em.c("s_sc")

        for g3 in range(4):
            o0 = g3 * 256
            p = g3 % 2
            gte = em.g(("slab", p))
            wsem = "s_w%d" % p
            dmaev("gpsimd", AR[p][:].rearrange("p (c o) -> p c o", c=32),
                  w3T[:, o0:o0 + 256].rearrange("(c p) o -> p c o", p=128),
                  wsem, waits=[gte] if gte else [])
            wdone = (wsem, em.c(wsem))
            for u in range(2):
                oc = 2 * g3 + u
                for rs in range(2):
                    b, f = fgroup(
                        [_ar(p, 0, ic, u) for ic in range(32)],
                        [_h(hA_fl, ic, rs * 512, (rs + 1) * 512)
                         for ic in range(32)], 32,
                        extra_waits=[wdone, ("s_sc", H2_SC)])
                    em.add("vector",
                           (lambda F_, B_, S_: (lambda e: e.tensor_scalar_mul(
                               F_, B_, S_)))(
                               _h(hB_fl, oc, rs * 512, (rs + 1) * 512),
                               b[:], s3_sb[:, oc:oc + 1]),
                           waits=[("s_f", f)], inc="s_ve")
                    em.setg(("FB", (rot[0] - 1) % 4), "s_ve")
            em.setg(("slab", p), "s_f")
        W3_VE = em.c("s_ve")
        for tt in range(8):
            t = hf * 8 + tt
            for c in range(DCH):
                w = []
                if c == 0:
                    w = [("s_ve", W3_VE), em.g(("TP",))]
                em.add("tensor",
                       (lambda O_, I_, N_: (lambda e: e.transpose(O_, I_, N_)))(
                           TP_ps[:, c * 128:(c + 1) * 128],
                           _h(hB_fl, c, tt * 128, (tt + 1) * 128), idn_sb[:]),
                       waits=w)
            for c in range(DCH):
                w = [em.g(("LT",))] if c == 0 else []
                em.add("tensor",
                       (lambda O_, I_, N_: (lambda e: e.transpose(O_, I_, N_)))(
                           LT_ps[:, c * 128:(c + 1) * 128],
                           x1_fm[:, c, r0 + tt * 128:r0 + (tt + 1) * 128],
                           idn_sb[:]),
                       waits=w, inc=("s_l2T" if c == DCH - 1 else None))
            l2t = em.c("s_l2T")
            gz = em.g(("z2r",))
            em.add("vector", cp(z2_sb[:], TP_ps[:, 0:1024]),
                   waits=[("s_l2T", l2t)] + ([gz] if gz else []))
            em.add("vector",
                   (lambda Z_, L_: (lambda e: e.tensor_add(Z_, Z_, L_)))(
                       z2_sb[:], LT_ps[:, 0:1024]),
                   inc="s_ve")
            em.setg(("TP",), "s_ve")
            em.setg(("LT",), "s_ve")
            em.add("vector",
                   (lambda Z_, M_: (lambda e: e.reduce_sum(M_, Z_, axis=AX.X)))(
                       z2_sb[:], m2_sb[:]),
                   inc="s_ve")
            zs2 = em.c("s_ve")
            em.add("scalar",
                   (lambda M_: (lambda e: e.mul(M_, M_, 1.0 / D)))(m2_sb[:]),
                   waits=[("s_ve", zs2)], inc="s_sc")
            msc = em.c("s_sc")
            em.add("scalar",
                   (lambda Q_, Z_, M_, V_: (lambda e: e.activation(
                       Q_, Z_, AF.Square, bias=M_, scale=-1.0, accum_out=V_)))(
                       hB_fl[:, 8192:9216], z2_sb[:], m2_sb[:], v2_sb[:]),
                   waits=[("s_sc", msc)])
            em.add("scalar",
                   (lambda V_, S_, E_: (lambda e: e.activation(S_, V_, AF.Sqrt,
                                                               bias=E_,
                                                               scale=1.0 / D)))(
                       v2_sb[:], sd2_sb[:], eps_sb[:]),
                   inc="s_sc")
            em.setg(("z2r",), "s_sc")
            em.add("vector",
                   (lambda S_, R_: (lambda e: e.reciprocal(R_, S_)))(
                       sd2_sb[:], rstd2_sb[:]),
                   waits=[("s_sc", em.c("s_sc"))], inc="s_vch")
            g = em.g(("outst",))
            em.add("vector",
                   (lambda Z_, M_, R_, O_: (lambda e: e.tensor_scalar(
                       O_, Z_, M_, R_, ALU.subtract, ALU.mult)))(
                       z2_sb[:], m2_sb[:], rstd2_sb[:], out_st[0][:]),
                   waits=([g] if g else []) + [("s_vch", em.c("s_vch"))], inc="s_ve")
            dmaev("scalar", out[t * 128:(t + 1) * 128, :], out_st[0][:],
                  "s_out", waits=[("s_ve", em.c("s_ve"))])
            em.setg(("outst",), "s_out")

    sems["s_lnT"] = nc.alloc_semaphore("s_lnT")

    ffn_half(0)
    ffn_half(1)

    em.add("scalar", (lambda e: e.nop()), waits=[("s_out", em.c("s_out"))])
    if dbg:
        em.add("sync", (lambda e: e.nop()), waits=[("s_dbg", em.c("s_dbg"))])

    stkF.close()
    stk.close()

    with nc.Block() as block:
        def body(engname):
            evs = em.ev[engname]
            chain = {"vector": "s_vch", "scalar": "s_sch"}.get(engname)

            def run(eng):
                seen = {}
                cnt = {}
                prev = None
                for fn, waits, inc, n in evs:
                    for s, v in waits:
                        if v > 0 and seen.get(s, 0) < v:
                            eng.wait_ge(sems[s], v)
                            seen[s] = v
                    if chain and n == 1 and prev is not None:
                        s, v = prev
                        if seen.get(s, 0) < v:
                            eng.wait_ge(sems[s], v)
                            seen[s] = v
                    ins = fn(eng)
                    use = inc if inc else (chain if (chain and n == 1) else None)
                    if use:
                        ins.then_inc(sems[use], n)
                        cnt[use] = cnt.get(use, 0) + n
                        if chain and n == 1:
                            prev = (use, cnt[use])
            return run

        block.sync(body("sync"))
        block.vector(body("vector"))
        block.scalar(body("scalar"))
        block.tensor(body("tensor"))
        block.gpsimd(body("gpsimd"))
    return nc


_BF = ml_dtypes.bfloat16


def _host_inputs(x, W, w1_w, w1_s, w2_w, w2_s, h1_w, h1_s, h2_w, h2_s, w3_w, w3_s):
    x = np.asarray(x, np.float32).reshape(SEQ, D)
    W = np.asarray(W, np.float32)
    shared = {
        "vmask": np.where((np.arange(SEQ)[None, :] // 128)
                          >= np.arange(128)[:, None], NEG, 0.0).astype(_BF),
        "tri": np.where(np.arange(128)[None, :] <= np.arange(128)[:, None],
                        0.0, NEG).astype(_BF),
        "idn": np.eye(128).astype(_BF),
        "wkT": np.ascontiguousarray(W[:D].T),
        "wvT": np.ascontiguousarray(W[D:].T),
        "w1T": np.ascontiguousarray(np.asarray(w1_w, np.float32).T),
        "w2T": np.ascontiguousarray(np.asarray(w2_w, np.float32).T),
        "h1T": np.ascontiguousarray(np.asarray(h1_w, np.float32).T),
        "h2T": np.ascontiguousarray(np.asarray(h2_w, np.float32).T),
        "w3T": np.ascontiguousarray(np.asarray(w3_w, np.float32).T),
        "s1i": np.ascontiguousarray(np.asarray(w1_s, np.float32).reshape(32, 128).T),
        "s2i": np.ascontiguousarray(np.asarray(w2_s, np.float32).reshape(32, 128).T),
        "sh1i": np.ascontiguousarray(np.asarray(h1_s, np.float32).reshape(32, 128).T),
        "sh2i": np.ascontiguousarray(np.asarray(h2_s, np.float32).reshape(32, 128).T),
        "s3i": np.ascontiguousarray(np.asarray(w3_s, np.float32).reshape(8, 128).T),
        "epsi": np.full((128, 1), 1e-5, np.float32),
    }
    maps = []
    for c in range(NCORES):
        tiles = [8 * s + c for s in range(NT)]
        rows = np.concatenate([np.arange(g * 128, (g + 1) * 128) for g in tiles])
        xq_h = np.ascontiguousarray(x[rows])
        u = np.zeros((128, 2048), np.float32)
        for s, g in enumerate(tiles):
            u[g, s * 128:(s + 1) * 128] = 1.0
        maps.append(dict(shared,
                         xqT=np.ascontiguousarray(xq_h.T),
                         umask=u.astype(_BF)))
    return maps


def kernel(L0_attn_x, L0_attn_W, L0_ffn_w1_w, L0_ffn_w1_s, L0_ffn_w2_w,
           L0_ffn_w2_s, L0_ffn_h1_w, L0_ffn_h1_s, L0_ffn_h2_w, L0_ffn_h2_s,
           L0_ffn_w3_w, L0_ffn_w3_s):
    maps = _host_inputs(L0_attn_x, L0_attn_W, L0_ffn_w1_w, L0_ffn_w1_s,
                        L0_ffn_w2_w, L0_ffn_w2_s, L0_ffn_h1_w, L0_ffn_h1_s,
                        L0_ffn_h2_w, L0_ffn_h2_s, L0_ffn_w3_w, L0_ffn_w3_s)
    nc = build_nc()
    res = run_bass_kernel_spmd(nc, maps, core_ids=list(range(NCORES)))
    full = np.empty((SEQ, D), np.float32)
    for c in range(NCORES):
        o = res.results[c]["out"]
        for s in range(NT):
            g = 8 * s + c
            full[g * 128:(g + 1) * 128] = o[s * 128:(s + 1) * 128]
    return full.reshape(1, SEQ, D)


# version 11
# speedup vs baseline: 1.2861x; 1.0070x over previous
import numpy as np
import ml_dtypes
from contextlib import ExitStack

import concourse.bass as bass
import concourse.mybir as mybir
from concourse.bass_utils import run_bass_kernel_spmd

F32 = mybir.dt.float32
BF16 = mybir.dt.bfloat16
AF = mybir.ActivationFunctionType
ALU = mybir.AluOpType
AX = mybir.AxisListType

SEQ = 16384
D = 1024
NCORES = 8
NT = 16
NBLK = 32
BK = 512
DCH = 8
DH = 4096
NEG = -30000.0

ENGS = ("sync", "vector", "scalar", "tensor", "gpsimd")


class _Em:

    def __init__(self):
        self.ev = {e: [] for e in ENGS}
        self.cnt = {}
        self.gate = {}

    def c(self, s):
        return self.cnt.get(s, 0)

    def add(self, eng, fn, waits=(), inc=None, dma=False):
        n = 16 if dma else 1
        if inc:
            self.cnt[inc] = self.c(inc) + n
        self.ev[eng].append((fn, tuple(w for w in waits if w is not None), inc, n))
        return self.c(inc) if inc else None

    def g(self, key):
        return self.gate.get(key)

    def setg(self, key, sem):
        self.gate[key] = (sem, self.c(sem))


def _parts(j):
    return [t for t in range(NT) if j < 2 * t + 2]


def build_nc(dbg=False, nblk=NBLK, serial=False):
    nc = bass.Bass("TRN2", target_bir_lowering=False, debug=False,
                   detect_race_conditions=False)
    dram = nc.dram_tensor

    xqT = dram("xqT", [D, 2048], F32, kind="ExternalInput").ap()
    cc_src = [dram("ccs%d" % s, [262144], BF16, kind="Internal").ap()
              for s in range(NT)]
    cc_dst = [dram("ccd%d" % s, [8, 262144], BF16, kind="Internal").ap()
              for s in range(NT)]
    umask = dram("umask", [128, 2048], BF16, kind="ExternalInput").ap()
    vmask = dram("vmask", [128, SEQ], BF16, kind="ExternalInput").ap()
    tri = dram("tri", [128, 128], BF16, kind="ExternalInput").ap()
    idn = dram("idn", [128, 128], BF16, kind="ExternalInput").ap()
    wkT = dram("wkT", [D, D], F32, kind="ExternalInput").ap()
    wvT = dram("wvT", [D, D], F32, kind="ExternalInput").ap()
    w1T = dram("w1T", [D, DH], F32, kind="ExternalInput").ap()
    w2T = dram("w2T", [D, DH], F32, kind="ExternalInput").ap()
    h1T = dram("h1T", [DH, DH], F32, kind="ExternalInput").ap()
    h2T = dram("h2T", [DH, DH], F32, kind="ExternalInput").ap()
    w3T = dram("w3T", [DH, D], F32, kind="ExternalInput").ap()
    s1i = dram("s1i", [128, 32], F32, kind="ExternalInput").ap()
    s2i = dram("s2i", [128, 32], F32, kind="ExternalInput").ap()
    sh1i = dram("sh1i", [128, 32], F32, kind="ExternalInput").ap()
    sh2i = dram("sh2i", [128, 32], F32, kind="ExternalInput").ap()
    s3i = dram("s3i", [128, 8], F32, kind="ExternalInput").ap()
    epsi = dram("epsi", [128, 1], F32, kind="ExternalInput").ap()
    out = dram("out", [2048, D], F32, kind="ExternalOutput").ap()
    if dbg:
        dbg_x1 = dram("dbg_x1", [128, DCH * 2048], BF16, kind="ExternalOutput").ap()
        dbg_O = dram("dbg_O", [128, NT * 1024], F32, kind="ExternalOutput").ap()
        dbg_s = dram("dbg_s", [128, NT], F32, kind="ExternalOutput").ap()

    sems = {}
    for s in ("s_in", "s_inw", "s_xk0", "s_xk1", "s_vm0", "s_vm1",
              "s_kvg", "s_kve", "s_S",
              "s_vstat", "s_exp", "s_T", "s_PTc", "s_PV", "s_Oupd",
              "s_ln0", "s_ln1", "s_lnv", "s_lns", "s_cc", "s_pk0", "s_pk1",
              "s_kv0", "s_kv1", "s_kv2", "s_kv3", "s_vm2", "s_vm3",
              "s_w0", "s_w1", "s_f", "s_sc", "s_ve", "s_l2T", "s_out", "s_dbg",
              "s_vch", "s_sch"):
        sems[s] = nc.alloc_semaphore(s)

    pa = nc.alloc_psum_tensor
    S_ps = [pa("S0", [128, 512], F32), pa("S1", [128, 512], F32)]
    KV_ps = [pa("KVA", [128, 512], F32), pa("KVB", [128, 512], F32)]
    PV_ps = pa("PVp", [128, 1024], F32)
    SB4 = None
    TP_ps = pa("TPp", [128, 1024], BF16)
    LT_ps = pa("LTp", [128, 1024], BF16)
    FB = [S_ps[0], S_ps[1], KV_ps[0], KV_ps[1]]
    SB4 = FB

    em = _Em()

    def mm(psl, lhsT, rhs, start, stop):
        return (lambda P_, L_, R_, st_, sp_:
                (lambda e: e.matmul(P_, L_, R_, start=st_, stop=sp_)))(
                    psl, lhsT, rhs, start, stop)

    def cp(dst, src):
        return (lambda D_, S_: (lambda e: e.tensor_copy(D_, S_)))(dst, src)

    def dmaev(eng, dst, src, inc, waits=()):
        em.add(eng, (lambda d, s: (lambda e: e.dma_start(out=d, in_=s)))(dst, src),
               waits=waits, inc=inc, dma=True)

    stk = ExitStack()

    def sb(name, shape, dt, st=stk, side=None):
        kw = {} if side is None else {"side": side}
        return st.enter_context(nc.sbuf_tensor(name, shape, dt, **kw))

    idn_sb = sb("idn_sb", [128, 128], BF16)
    s1_sb = sb("s1_sb", [128, 32], F32)
    s2_sb = sb("s2_sb", [128, 32], F32)
    sh1_sb = sb("sh1_sb", [128, 32], F32)
    sh2_sb = sb("sh2_sb", [128, 32], F32)
    s3_sb = sb("s3_sb", [128, 8], F32)
    eps_sb = sb("eps_sb", [128, 1], F32)

    stkA = ExitStack()
    O_sb = sb("O_sb", [128, NT, 1024], F32, stkA)
    x_fm = sb("x_fm", [128, DCH, 2048], BF16, stkA)
    s_sb = sb("s_sb", [128, NT], F32, stkA)
    nm_sb = sb("nm_sb", [128, NT], F32, stkA)
    d_sb = sb("d_sb", [128, 4], F32, stkA)
    ct_sb = sb("ct_sb", [128, 4], F32, stkA)
    sbk_sb = sb("sbk_sb", [128, 4], F32, stkA)
    bmax_sb = sb("bmax_sb", [128, 4], F32, stkA)

    stkB = ExitStack()
    u_sb = sb("u_sb", [128, 2048], BF16, stkB)
    wk_sb = sb("wk_sb", [128, DCH, 1024], BF16, stkB)
    wv_sb = sb("wv_sb", [128, DCH, 1024], BF16, stkB)
    K_sb = [sb("K%d" % i, [128, DCH, BK], BF16, stkB) for i in range(4)]
    V_sb = [sb("V%d" % i, [128, 4, 1024], BF16, stkB) for i in range(4)]
    vm_sb = [sb("vm%d" % i, [128, BK], BF16, stkB) for i in range(2)]
    _kt = sb("Kt0", [128, DCH, 128], BF16, stkB)
    _vt = sb("Vt0", [128, 1024], BF16, stkB)
    Kt_sb = [_kt, _kt]
    Vt_sb = [_vt, _vt]
    P_sb = [sb("P0", [128, 512], BF16, stkB), sb("P1", [128, 512], BF16, stkB)]
    PT_sb = [sb("PT0", [128, 512], BF16, stkB), sb("PT1", [128, 512], BF16, stkB)]
    tri_sb = sb("tri_sb", [128, 128], BF16, stkB)

    dmaev("sync", idn_sb[:], idn, "s_in")
    dmaev("sync", tri_sb[:], tri, "s_in")
    dmaev("sync", u_sb[:], umask, "s_in")
    dmaev("sync", s1_sb[:], s1i, "s_in")
    dmaev("sync", s2_sb[:], s2i, "s_in")
    dmaev("sync", sh1_sb[:], sh1i, "s_in")
    dmaev("sync", sh2_sb[:], sh2i, "s_in")
    dmaev("sync", s3_sb[:], s3i, "s_in")
    dmaev("sync", eps_sb[:], epsi, "s_in")
    SETUP_SYNC = em.c("s_in")

    dmaev("gpsimd", x_fm[:, :, 0:256],
          xqT[:, 0:256].rearrange("(c p) q -> p c q", p=128), "s_inw")
    dmaev("gpsimd", wk_sb[:], wkT.rearrange("(c p) o -> p c o", p=128), "s_inw")
    dmaev("gpsimd", wv_sb[:], wvT.rearrange("(c p) o -> p c o", p=128), "s_inw")
    SETUP_GP_A = em.c("s_inw")
    dmaev("gpsimd", x_fm[:, :, 256:2048],
          xqT[:, 256:2048].rearrange("(c p) q -> p c q", p=128), "s_inw")
    SETUP_GP = em.c("s_inw")

    VM_LOADED = {}
    KVE_BLK = {}

    def load_vm(j):
        g = em.g(("vm", j % 2))
        sname = "s_vm%d" % (j % 2)
        dmaev("gpsimd", vm_sb[j % 2][:], vmask[:, j * BK:(j + 1) * BK],
              sname, waits=([g] if g else []))
        VM_LOADED[j] = (sname, em.c(sname))

    def load_kv(j):
        s, cb = j // 2, 4 * (j % 2)
        pr = j % 4
        sname = "s_kv%d" % pr
        gk = em.g(("Kpar", pr))
        gv = em.g(("Vpar", pr))
        w = [("s_cc", s + 1)] + [x for x in (gk, gv) if x]
        dmaev("gpsimd",
              K_sb[pr][:].rearrange("p c k -> p c k"),
              cc_dst[s][cb:cb + 4, 0:131072].rearrange(
                  "t (p c k) -> p c t k", p=128, c=DCH),
              sname, waits=w)
        dmaev("gpsimd",
              V_sb[pr][:],
              cc_dst[s][cb:cb + 4, 131072:262144].rearrange(
                  "t (p v) -> p t v", p=128),
              sname)
        KVE_BLK[j] = (sname, em.c(sname))

    load_vm(0)
    load_vm(1)

    rg8 = [list(range(8))]
    for t in range(NT):
        pr = t % 2
        for g in range(10):
            bank = KV_ps[g % 2]
            if g < 8:
                c8 = g
                lhss = [wk_sb[:, ic, c8 * 128:(c8 + 1) * 128] for ic in range(DCH)]
                rhss = [x_fm[:, ic, t * 128:(t + 1) * 128] for ic in range(DCH)]
                po = bank[:, 0:128]
                dst = Kt_sb[pr][:, c8, :]
                gates = [em.g(("Ktpar", 0)), em.g(("Ktpk", 0))]
            else:
                h = g - 8
                lhss = [x_fm[:, ic, t * 128:(t + 1) * 128] for ic in range(DCH)]
                rhss = [wv_sb[:, ic, h * 512:(h + 1) * 512] for ic in range(DCH)]
                po = bank[:]
                dst = Vt_sb[pr][:, h * 512:(h + 1) * 512]
                gates = [em.g(("Vtpar", 0)), em.g(("Vtpk", 0))]
            for ic in range(DCH):
                w = []
                if ic == 0:
                    w = [("s_inw", SETUP_GP_A if t < 2 else SETUP_GP),
                         em.g(("KVps", g % 2))]
                em.add("tensor", mm(po, lhss[ic], rhss[ic], ic == 0, ic == DCH - 1),
                       waits=w, inc=("s_kvg" if ic == DCH - 1 else None))
            w = [("s_kvg", em.c("s_kvg"))] + gates
            em.add("vector", cp(dst, po), waits=w, inc="s_kve")
            em.setg(("KVps", g % 2), "s_kve")
        KV_T = em.c("s_kve")
        pks = "s_pk%d" % pr
        dmaev("gpsimd", cc_src[t][0:131072].rearrange("(p c k) -> p c k",
                                                      p=128, c=DCH),
              Kt_sb[pr][:], pks, waits=[("s_kve", KV_T)])
        dmaev("gpsimd", cc_src[t][131072:262144].rearrange("(p v) -> p v", p=128),
              Vt_sb[pr][:], pks)
        pkv = em.c(pks)
        em.add("gpsimd",
               (lambda S_, D_: (lambda e: e.collective_compute(
                   "AllGather", ALU.bypass, rg8, ins=[S_], outs=[D_])))(
                   cc_src[t], cc_dst[t]),
               waits=[(pks, pkv)], inc="s_cc")
        em.setg(("Ktpk", 0), pks)
        em.setg(("Vtpk", 0), pks)
        if nblk > 0 and t == 0:
            load_kv(0)
            if nblk > 1:
                load_kv(1)
        if nblk > 2 and t == 1:
            load_kv(2)
            if nblk > 3:
                load_kv(3)
        for ic in range(DCH + 1):
            w = []
            if ic == 0:
                w = [("s_kve", KV_T), em.g(("Spar", pr)), ("s_in", SETUP_SYNC)]
            if ic < DCH:
                lhsT = x_fm[:, ic, t * 128:(t + 1) * 128]
                rhs = Kt_sb[pr][:, ic, :]
            else:
                lhsT = idn_sb[:]
                rhs = tri_sb[:]
            em.add("tensor", mm(S_ps[pr][:, 0:128], lhsT, rhs,
                                ic == 0, ic == DCH),
                   waits=w, inc=("s_S" if ic == DCH else None))
        em.setg(("Ktpar", 0), "s_S")
        sS = em.c("s_S")
        em.add("vector",
               (lambda P_, M_: (lambda e: e.reduce_max(M_, P_, axis=AX.X,
                                                       negate=True)))(
                   S_ps[pr][:, 0:128], nm_sb[:, t:t + 1]),
               waits=[("s_S", sS)], inc="s_vstat")
        vst = em.c("s_vstat")
        em.add("scalar",
               (lambda Pp, Sp, B_, A_: (lambda e: e.activation(
                   Pp, Sp, AF.Exp, bias=B_, accum_out=A_)))(
                   P_sb[pr][:, 0:128], S_ps[pr][:, 0:128], nm_sb[:, t:t + 1],
                   s_sb[:, t:t + 1]),
               waits=[("s_vstat", vst), em.g(("Ppar", pr))], inc="s_exp")
        em.setg(("Spar", pr), "s_exp")
        sE = em.c("s_exp")
        em.add("tensor",
               (lambda O_, I_, N_: (lambda e: e.transpose(O_, I_, N_)))(
                   TP_ps[:, 0:128], P_sb[pr][:, 0:128], idn_sb[:]),
               waits=[("s_exp", sE), em.g(("TP",)), ("s_in", SETUP_SYNC)],
               inc="s_T")
        em.setg(("Ppar", pr), "s_T")
        em.add("vector", cp(PT_sb[pr][:, 0:128], TP_ps[:, 0:128]),
               waits=[("s_T", em.c("s_T")), em.g(("PTpar", pr))], inc="s_PTc")
        em.setg(("TP",), "s_PTc")
        ptc = em.c("s_PTc")
        for h in range(2):
            em.add("tensor",
                   mm(PV_ps[:, h * 512:(h + 1) * 512], PT_sb[pr][:, 0:128],
                      Vt_sb[pr][:, h * 512:(h + 1) * 512], True, True),
                   waits=([("s_PTc", ptc), em.g(("PVps",))] if h == 0 else ()),
                   inc=("s_PV" if h == 1 else None))
        em.setg(("PTpar", pr), "s_PV")
        em.setg(("Vtpar", 0), "s_PV")
        spv = em.c("s_PV")
        for h in range(2):
            em.add("vector", cp(O_sb[:, t, h * 512:(h + 1) * 512],
                                PV_ps[:, h * 512:(h + 1) * 512]),
                   waits=([("s_PV", spv)] if h == 0 else ()),
                   inc=("s_Oupd" if h == 1 else None))
        em.setg(("PVps",), "s_Oupd")

    def emit_ln1(t, owait, pr):
        zO = zo_sb[pr]
        for c in range(DCH):
            w = [em.g(("LT",))] if c == 0 else []
            em.add("tensor",
                   (lambda O_, I_, N_: (lambda e: e.transpose(O_, I_, N_)))(
                       LT_ps[:, c * 128:(c + 1) * 128],
                       x_fm[:, c, t * 128:(t + 1) * 128], idn_sb[:]),
                   waits=w, inc=("s_lnT" if c == DCH - 1 else None))
        xT_done = em.c("s_lnT")
        em.add("vector",
               (lambda S_, R_: (lambda e: e.reciprocal(R_, S_)))(
                   s_sb[:, t:t + 1], rs_sb[pr][:]),
               waits=owait, inc="s_vch")
        em.add("vector",
               (lambda O_, R_, Z_: (lambda e: e.tensor_scalar_mul(Z_, O_, R_)))(
                   O_sb[:, t, :], rs_sb[pr][:], zO[:]),
               waits=[("s_vch", em.c("s_vch"))])
        em.add("vector",
               (lambda L_, Z_, ZB: (lambda e: e.scalar_tensor_tensor(
                   ZB, L_, 2.0, Z_, ALU.mult, ALU.add)))(
                   LT_ps[:, 0:1024], zO[:], z_sb[pr][:]),
               waits=[("s_lnT", xT_done)], inc="s_lnv")
        em.setg(("LT",), "s_lnv")
        em.add("vector",
               (lambda Z_, M_: (lambda e: e.reduce_sum(M_, Z_, axis=AX.X)))(
                   z_sb[pr][:], mu_sb[pr][:]),
               inc="s_lnv")
        zs = em.c("s_lnv")
        em.add("scalar",
               (lambda M_: (lambda e: e.mul(M_, M_, 1.0 / D)))(mu_sb[pr][:]),
               waits=[("s_lnv", zs)], inc="s_lns")
        em.add("scalar",
               (lambda Q_, Z_, M_, V_: (lambda e: e.activation(
                   Q_, Z_, AF.Square, bias=M_, scale=-1.0, accum_out=V_)))(
                   zo_sb[pr][:], z_sb[pr][:], mu_sb[pr][:], vs_sb[pr][:]))
        em.add("scalar",
               (lambda V_, S_, E_: (lambda e: e.activation(S_, V_, AF.Sqrt,
                                                           bias=E_, scale=1.0 / D)))(
                   vs_sb[pr][:], sd_sb[pr][:], eps_sb[:]),
               waits=[("s_in", SETUP_SYNC)], inc="s_lns")
        em.add("vector",
               (lambda S_, R_: (lambda e: e.reciprocal(R_, S_)))(
                   sd_sb[pr][:], rstd_sb[pr][:]),
               waits=[("s_lns", em.c("s_lns"))], inc="s_vch")
        em.add("vector",
               (lambda Z_, M_, R_, X_: (lambda e: e.tensor_scalar(
                   X_, Z_, M_, R_, ALU.subtract, ALU.mult)))(
                   z_sb[pr][:], mu_sb[pr][:], rstd_sb[pr][:], x1b_sb[pr][:]),
               waits=[("s_vch", em.c("s_vch"))], inc="s_lnv")
        x1done = em.c("s_lnv")
        for c in range(DCH):
            w = [em.g(("TP",)), ("s_lnv", x1done)] if c == 0 else []
            em.add("tensor",
                   (lambda O_, I_, N_: (lambda e: e.transpose(O_, I_, N_)))(
                       TP_ps[:, c * 128:(c + 1) * 128],
                       x1b_sb[pr][:, c * 128:(c + 1) * 128], idn_sb[:]),
                   waits=w, inc=("s_lnT" if c == DCH - 1 else None))
        for half in range(2):
            em.add("vector",
                   (lambda D2_, S2_: (lambda e: e.tensor_copy(D2_, S2_)))(
                       x1_fm[:, half * 4:(half + 1) * 4, t * 128:(t + 1) * 128],
                       TP_ps[:, half * 512:(half + 1) * 512].rearrange(
                           "p (c q) -> p c q", c=4)),
                   waits=([("s_lnT", em.c("s_lnT"))] if half == 0 else ()),
                   inc=("s_lnv" if half == 1 else None))
        em.setg(("TP",), "s_lnv")

    jobq = [(j, t) for j in range(nblk) for t in _parts(j)]
    NJOBS = len(jobq)
    n_start = {}
    n_end = {}
    idx = 0
    for j in range(nblk):
        n_start[j] = idx
        idx += len(_parts(j))
        n_end[j] = idx
    S_OF, VST_OF, EXP_OF, T_OF, PTC_OF, PV_OF, OUP_OF = {}, {}, {}, {}, {}, {}, {}

    em.setg(("Spar", 2), "s_kve")
    em.setg(("Spar", 3), "s_kve")
    TILE_LAST_EXP = {}

    def emit_S(n):
        j, t = jobq[n]
        pr = j % 4
        bank = SB4[n % 4]
        for ic in range(DCH + 1):
            if ic == 0:
                w = [KVE_BLK[j], em.g(("Spar", n % 4))]
                if serial:
                    w += [(s_, em.c(s_)) for s_ in
                          ("s_Oupd", "s_exp", "s_PTc", "s_kve", "s_vstat")]
            elif ic == DCH:
                w = [VM_LOADED[j], ("s_in", SETUP_SYNC)]
            else:
                w = []
            if ic < DCH:
                lhsT = x_fm[:, ic, t * 128:(t + 1) * 128]
                rhs = K_sb[pr][:, ic, :]
            else:
                lhsT = u_sb[:, t * 128:(t + 1) * 128]
                rhs = vm_sb[j % 2][:]
            em.add("tensor", mm(bank[:], lhsT, rhs, ic == 0, ic == DCH),
                   waits=w, inc=("s_S" if ic == DCH else None))
        S_OF[n] = em.c("s_S")
        if n + 1 == n_end[j]:
            em.setg(("Kpar", j % 4), "s_S")
            em.setg(("vm", j % 2), "s_S")
            if j + 2 < nblk:
                load_vm(j + 2)

    def emit_stats(n):
        j, t = jobq[n]
        q = n % 4
        bank = SB4[q]
        em.add("vector",
               (lambda B_, M_: (lambda e: e.reduce_max(M_, B_, axis=AX.X,
                                                       negate=True)))(
                   bank[:], bmax_sb[:, q:q + 1]),
               waits=[("s_S", S_OF[n])], inc="s_vch")
        em.add("vector",
               (lambda B_, NM_, D_: (lambda e: e.tensor_scalar(
                   D_, B_, NM_, 0.0, ALU.subtract, ALU.min)))(
                   bmax_sb[:, q:q + 1], nm_sb[:, t:t + 1], d_sb[:, q:q + 1]),
               waits=[("s_vch", em.c("s_vch"))])
        em.add("vector",
               (lambda NM_, D_: (lambda e: e.tensor_scalar_add(NM_, NM_, D_)))(
                   nm_sb[:, t:t + 1], d_sb[:, q:q + 1]),
               waits=([("s_exp", TILE_LAST_EXP[t])] if t in TILE_LAST_EXP else []),
               inc="s_vstat")
        VST_OF[n] = em.c("s_vstat")
        em.add("scalar",
               (lambda D_, C_: (lambda e: e.activation(C_, D_, AF.Exp)))(
                   d_sb[:, q:q + 1], ct_sb[:, q:q + 1]),
               waits=[("s_vstat", VST_OF[n])])
        em.add("scalar",
               (lambda Pp, Sp, B_, A_: (lambda e: e.activation(
                   Pp, Sp, AF.Exp, bias=B_, accum_out=A_)))(
                   P_sb[n % 2][:], bank[:], nm_sb[:, t:t + 1], sbk_sb[:, q:q + 1]),
               waits=[em.g(("Ppar", n % 2))], inc="s_exp")
        em.setg(("Spar", q), "s_exp")
        EXP_OF[n] = em.c("s_exp")
        TILE_LAST_EXP[t] = em.c("s_exp")

    def emit_supd(n):
        j, t = jobq[n]
        q = n % 4
        em.add("vector",
               (lambda S_, C_, B_: (lambda e: e.scalar_tensor_tensor(
                   S_, S_, C_, B_, ALU.mult, ALU.add)))(
                   s_sb[:, t:t + 1], ct_sb[:, q:q + 1], sbk_sb[:, q:q + 1]),
               waits=[("s_exp", EXP_OF[n])])

    def emit_T(n):
        q = n % 2
        for kc in range(4):
            w = []
            if kc == 0:
                w = [("s_exp", EXP_OF[n]), em.g(("TP",))]
            em.add("tensor",
                   (lambda O_, I_, N_: (lambda e: e.transpose(O_, I_, N_)))(
                       TP_ps[:, kc * 128:(kc + 1) * 128],
                       P_sb[q][:, kc * 128:(kc + 1) * 128], idn_sb[:]),
                   waits=w, inc=("s_T" if kc == 3 else None))
        em.setg(("Ppar", q), "s_T")
        T_OF[n] = em.c("s_T")
        em.add("scalar",
               (lambda S_, D_: (lambda e: e.copy(D_, S_)))(
                   TP_ps[:, 0:512], PT_sb[q][:]),
               waits=[("s_T", T_OF[n]), em.g(("PTpar", q))], inc="s_PTc")
        em.setg(("TP",), "s_PTc")
        PTC_OF[n] = em.c("s_PTc")

    def emit_PV(n):
        j, t = jobq[n]
        pr = j % 4
        q4 = n % 4
        q = n % 2
        for h in range(2):
            for kc in range(4):
                w = []
                if h == 0 and kc == 0:
                    w = [("s_PTc", PTC_OF[n]), em.g(("PVps",))]
                em.add("tensor",
                       mm(PV_ps[:, h * 512:(h + 1) * 512],
                          PT_sb[q][:, kc * 128:(kc + 1) * 128],
                          V_sb[pr][:, kc, h * 512:(h + 1) * 512],
                          kc == 0, kc == 3),
                       waits=w, inc=("s_PV" if (h == 1 and kc == 3) else None))
        em.setg(("PTpar", q), "s_PV")
        if n + 1 == n_end[j]:
            em.setg(("Vpar", j % 4), "s_PV")
        PV_OF[n] = em.c("s_PV")
        for h in range(2):
            em.add("vector",
                   (lambda O_, C_, Pp: (lambda e: e.scalar_tensor_tensor(
                       O_, O_, C_, Pp, ALU.mult, ALU.add)))(
                       O_sb[:, t, h * 512:(h + 1) * 512], ct_sb[:, q4:q4 + 1],
                       PV_ps[:, h * 512:(h + 1) * 512]),
                   waits=([("s_PV", PV_OF[n])] if h == 0 else ()),
                   inc=("s_Oupd" if h == 1 else None))
        em.setg(("PVps",), "s_Oupd")
        OUP_OF[n] = em.c("s_Oupd")

    emitted = [0]

    def try_emit_S():
        if emitted[0] < NJOBS:
            jn, _ = jobq[emitted[0]]
            if jn in KVE_BLK:
                emit_S(emitted[0])
                emitted[0] += 1

    try_emit_S()
    emit_stats(0)
    try_emit_S()
    if NJOBS > 1:
        emit_stats(1)
    for n in range(NJOBS):
        j, t = jobq[n]
        try_emit_S()
        emit_T(n)
        if n + 2 < NJOBS:
            emit_stats(n + 2)
        emit_PV(n)
        emit_supd(n)
        if n + 1 == n_end[j] and j + 4 < nblk:
            load_kv(j + 4)

    ATT_END = [("s_S", em.c("s_S")), ("s_PV", em.c("s_PV")),
               ("s_Oupd", em.c("s_Oupd")), ("s_exp", em.c("s_exp")),
               ("s_PTc", em.c("s_PTc")), ("s_kve", em.c("s_kve")),
               ("s_vstat", em.c("s_vstat"))]


    stkB.close()
    x1_fm = sb("x1_fm", [128, DCH, 2048], BF16, side="right")
    stkL = ExitStack()
    z_sb = [sb("z0", [128, D], BF16, stkL), sb("z1", [128, D], BF16, stkL)]
    zo_sb = [sb("zo0", [128, D], F32, stkL), sb("zo1", [128, D], F32, stkL)]
    x1b_sb = [sb("x1b0", [128, D], BF16, stkL), sb("x1b1", [128, D], BF16, stkL)]
    rs_sb = [sb("rs0", [128, 1], F32, stkL), sb("rs1", [128, 1], F32, stkL)]
    mu_sb = [sb("mu0", [128, 1], F32, stkL), sb("mu1", [128, 1], F32, stkL)]
    vs_sb = [sb("vs0", [128, 1], F32, stkL), sb("vs1", [128, 1], F32, stkL)]
    sd_sb = [sb("sd0", [128, 1], F32, stkL), sb("sd1", [128, 1], F32, stkL)]
    rstd_sb = [sb("rstd0", [128, 1], F32, stkL), sb("rstd1", [128, 1], F32, stkL)]
    for t in range(NT):
        emit_ln1(t, list(ATT_END) if t == 0 else [], t % 2)
    LN_DONE = [("s_lnv", em.c("s_lnv")), ("s_lns", em.c("s_lns")),
               ("s_lnT", em.c("s_lnT")),
               ("s_ln0", em.c("s_ln0")), ("s_ln1", em.c("s_ln1"))]

    if dbg:
        dmaev("sync", dbg_x1, x1_fm[:].rearrange("p c q -> p (c q)"), "s_dbg",
              waits=list(LN_DONE))
        dmaev("sync", dbg_O, O_sb[:].rearrange("p t d -> p (t d)"), "s_dbg",
              waits=list(ATT_END))
        dmaev("sync", dbg_s, s_sb[:], "s_dbg", waits=list(ATT_END))

    stkL.close()
    stkA.close()
    stkF = ExitStack()
    hA_fl = sb("hA", [128, 32 * 1024], BF16, stkF)
    hB_fl = sb("hB", [128, 32 * 1024], BF16, stkF)
    AR = [sb("ar0", [128, 8192], BF16, stkF), sb("ar1", [128, 8192], BF16, stkF)]
    asl_sb = [sb("asl0", [128, 512], BF16, stkF), sb("asl1", [128, 512], BF16, stkF)]
    z2_sb = sb("z2_sb", [128, D], BF16, stkF)
    m2_sb = sb("m2_sb", [128, 1], F32, stkF)
    v2_sb = sb("v2_sb", [128, 1], F32, stkF)
    sd2_sb = sb("sd2_sb", [128, 1], F32, stkF)
    rstd2_sb = sb("rstd2_sb", [128, 1], F32, stkF)
    out_st = [sb("outst0", [128, D], F32, stkF)]
    def _h(t_, c, a, b):
        return t_[:, c * 1024 + a:c * 1024 + b]

    def _ar(p, base, ic, u):
        return AR[p][:, base + ic * 256 + u * 128:base + ic * 256 + (u + 1) * 128]

    rot = [0]
    _dbgw = [("s_dbg", em.c("s_dbg"))] if dbg else []
    first_ffn = {"gpsimd": list(ATT_END) + list(LN_DONE) + _dbgw,
                 "tensor": list(LN_DONE) + _dbgw, "scalar": list(LN_DONE) + _dbgw,
                 "vector": list(LN_DONE) + _dbgw}

    def takef(eng, extra=()):
        w = first_ffn.get(eng)
        if w is not None:
            first_ffn[eng] = None
            return list(extra) + w
        return list(extra)

    def fgroup(lhss, rhss, nacc, extra_waits=()):
        b = rot[0] % 4
        rot[0] += 1
        bank = FB[b]
        for ic in range(nacc):
            w = []
            if ic == 0:
                w = list(extra_waits) + [em.g(("FB", b))]
                w = takef("tensor", w)
            em.add("tensor", mm(bank[:], lhss[ic], rhss[ic],
                                ic == 0, ic == nacc - 1),
                   waits=w, inc=("s_f" if ic == nacc - 1 else None))
        return bank, em.c("s_f")

    def ffn_half(hf):
        r0 = hf * 1024
        for sl in range(16):
            o0 = sl * 256
            p = sl % 2
            g = em.g(("slab", p))
            wsem = "s_w%d" % p
            for base, wt in ((0, w1T), (2048, w2T)):
                dmaev("gpsimd",
                      AR[p][:, base:base + 2048].rearrange("p (c o) -> p c o", c=8),
                      wt[:, o0:o0 + 256].rearrange("(c p) o -> p c o", p=128),
                      wsem, waits=takef("gpsimd", [g] if g else []))
                g = None
            wdone = (wsem, em.c(wsem))
            for u in range(2):
                oc = 2 * sl + u
                for rs in range(2):
                    rsl = slice(r0 + rs * 512, r0 + (rs + 1) * 512)
                    b1, f1 = fgroup(
                        [_ar(p, 0, ic, u) for ic in range(DCH)],
                        [x1_fm[:, ic, rsl] for ic in range(DCH)], DCH,
                        extra_waits=[wdone])
                    em.add("scalar",
                           (lambda A_, B_, S_: (lambda e: e.activation(
                               A_, B_, AF.Silu, scale=S_)))(
                               asl_sb[rot[0] % 2][:], b1[:], s1_sb[:, oc:oc + 1]),
                           waits=takef("scalar",
                                       [("s_f", f1), em.g(("asl", rot[0] % 2))]),
                           inc="s_sc")
                    em.setg(("FB", (rot[0] - 1) % 4), "s_sc")
                    sc1 = em.c("s_sc")
                    b2, f2 = fgroup(
                        [_ar(p, 2048, ic, u) for ic in range(DCH)],
                        [x1_fm[:, ic, rsl] for ic in range(DCH)], DCH,
                        extra_waits=[wdone])
                    em.add("vector",
                           (lambda H_, B_, S_, A_: (lambda e: e.scalar_tensor_tensor(
                               H_, B_, S_, A_, ALU.mult, ALU.mult)))(
                               _h(hA_fl, oc, rs * 512, (rs + 1) * 512), b2[:],
                               s2_sb[:, oc:oc + 1], asl_sb[(rot[0] - 1) % 2][:]),
                           waits=takef("vector", [("s_f", f2), ("s_sc", sc1)]),
                           inc="s_ve")
                    em.setg(("FB", (rot[0] - 1) % 4), "s_ve")
                    em.setg(("asl", (rot[0] - 1) % 2), "s_ve")
            em.setg(("slab", p), "s_f")
        A_VE = em.c("s_ve")

        def stage_h(wt, src_fl, dst_fl, ssb, src_gate):
            for sl in range(16):
                o0 = sl * 256
                p = sl % 2
                g = em.g(("slab", p))
                wsem = "s_w%d" % p
                dmaev("gpsimd", AR[p][:].rearrange("p (c o) -> p c o", c=32),
                      wt[:, o0:o0 + 256].rearrange("(c p) o -> p c o", p=128),
                      wsem, waits=[g] if g else [])
                wdone = (wsem, em.c(wsem))
                for u in range(2):
                    oc = 2 * sl + u
                    for rs in range(2):
                        b, f = fgroup(
                            [_ar(p, 0, ic, u) for ic in range(32)],
                            [_h(src_fl, ic, rs * 512, (rs + 1) * 512)
                             for ic in range(32)], 32,
                            extra_waits=[wdone] + src_gate)
                        em.add("scalar",
                               (lambda A_, B_, S_: (lambda e: e.activation(
                                   A_, B_, AF.Gelu, scale=S_)))(
                                   _h(dst_fl, oc, rs * 512, (rs + 1) * 512),
                                   b[:], ssb[:, oc:oc + 1]),
                               waits=[("s_f", f)], inc="s_sc")
                        em.setg(("FB", (rot[0] - 1) % 4), "s_sc")
                em.setg(("slab", p), "s_f")
        stage_h(h1T, hA_fl, hB_fl, sh1_sb, [("s_ve", A_VE)])
        H1_SC = em.c("s_sc")
        stage_h(h2T, hB_fl, hA_fl, sh2_sb, [("s_sc", H1_SC)])
        H2_SC = em.c("s_sc")

        for g3 in range(4):
            o0 = g3 * 256
            p = g3 % 2
            gte = em.g(("slab", p))
            wsem = "s_w%d" % p
            dmaev("gpsimd", AR[p][:].rearrange("p (c o) -> p c o", c=32),
                  w3T[:, o0:o0 + 256].rearrange("(c p) o -> p c o", p=128),
                  wsem, waits=[gte] if gte else [])
            wdone = (wsem, em.c(wsem))
            for u in range(2):
                oc = 2 * g3 + u
                for rs in range(2):
                    b, f = fgroup(
                        [_ar(p, 0, ic, u) for ic in range(32)],
                        [_h(hA_fl, ic, rs * 512, (rs + 1) * 512)
                         for ic in range(32)], 32,
                        extra_waits=[wdone, ("s_sc", H2_SC)])
                    em.add("vector",
                           (lambda F_, B_, S_: (lambda e: e.tensor_scalar_mul(
                               F_, B_, S_)))(
                               _h(hB_fl, oc, rs * 512, (rs + 1) * 512),
                               b[:], s3_sb[:, oc:oc + 1]),
                           waits=[("s_f", f)], inc="s_ve")
                    em.setg(("FB", (rot[0] - 1) % 4), "s_ve")
            em.setg(("slab", p), "s_f")
        W3_VE = em.c("s_ve")
        for tt in range(8):
            t = hf * 8 + tt
            for c in range(DCH):
                w = []
                if c == 0:
                    w = [("s_ve", W3_VE), em.g(("TP",))]
                em.add("tensor",
                       (lambda O_, I_, N_: (lambda e: e.transpose(O_, I_, N_)))(
                           TP_ps[:, c * 128:(c + 1) * 128],
                           _h(hB_fl, c, tt * 128, (tt + 1) * 128), idn_sb[:]),
                       waits=w)
            for c in range(DCH):
                w = [em.g(("LT",))] if c == 0 else []
                em.add("tensor",
                       (lambda O_, I_, N_: (lambda e: e.transpose(O_, I_, N_)))(
                           LT_ps[:, c * 128:(c + 1) * 128],
                           x1_fm[:, c, r0 + tt * 128:r0 + (tt + 1) * 128],
                           idn_sb[:]),
                       waits=w, inc=("s_l2T" if c == DCH - 1 else None))
            l2t = em.c("s_l2T")
            gz = em.g(("z2r",))
            em.add("vector", cp(z2_sb[:], TP_ps[:, 0:1024]),
                   waits=[("s_l2T", l2t)] + ([gz] if gz else []))
            em.add("vector",
                   (lambda Z_, L_: (lambda e: e.tensor_add(Z_, Z_, L_)))(
                       z2_sb[:], LT_ps[:, 0:1024]),
                   inc="s_ve")
            em.setg(("TP",), "s_ve")
            em.setg(("LT",), "s_ve")
            em.add("vector",
                   (lambda Z_, M_: (lambda e: e.reduce_sum(M_, Z_, axis=AX.X)))(
                       z2_sb[:], m2_sb[:]),
                   inc="s_ve")
            zs2 = em.c("s_ve")
            em.add("scalar",
                   (lambda M_: (lambda e: e.mul(M_, M_, 1.0 / D)))(m2_sb[:]),
                   waits=[("s_ve", zs2)], inc="s_sc")
            msc = em.c("s_sc")
            em.add("scalar",
                   (lambda Q_, Z_, M_, V_: (lambda e: e.activation(
                       Q_, Z_, AF.Square, bias=M_, scale=-1.0, accum_out=V_)))(
                       hB_fl[:, 8192:9216], z2_sb[:], m2_sb[:], v2_sb[:]),
                   waits=[("s_sc", msc)])
            em.add("scalar",
                   (lambda V_, S_, E_: (lambda e: e.activation(S_, V_, AF.Sqrt,
                                                               bias=E_,
                                                               scale=1.0 / D)))(
                       v2_sb[:], sd2_sb[:], eps_sb[:]),
                   inc="s_sc")
            em.setg(("z2r",), "s_sc")
            em.add("vector",
                   (lambda S_, R_: (lambda e: e.reciprocal(R_, S_)))(
                       sd2_sb[:], rstd2_sb[:]),
                   waits=[("s_sc", em.c("s_sc"))], inc="s_vch")
            g = em.g(("outst",))
            em.add("vector",
                   (lambda Z_, M_, R_, O_: (lambda e: e.tensor_scalar(
                       O_, Z_, M_, R_, ALU.subtract, ALU.mult)))(
                       z2_sb[:], m2_sb[:], rstd2_sb[:], out_st[0][:]),
                   waits=([g] if g else []) + [("s_vch", em.c("s_vch"))], inc="s_ve")
            dmaev("scalar", out[t * 128:(t + 1) * 128, :], out_st[0][:],
                  "s_out", waits=[("s_ve", em.c("s_ve"))])
            em.setg(("outst",), "s_out")

    sems["s_lnT"] = nc.alloc_semaphore("s_lnT")

    ffn_half(0)
    ffn_half(1)

    em.add("scalar", (lambda e: e.nop()), waits=[("s_out", em.c("s_out"))])
    if dbg:
        em.add("sync", (lambda e: e.nop()), waits=[("s_dbg", em.c("s_dbg"))])

    stkF.close()
    stk.close()

    with nc.Block() as block:
        def body(engname):
            evs = em.ev[engname]
            chain = {"vector": "s_vch", "scalar": "s_sch"}.get(engname)

            def run(eng):
                seen = {}
                cnt = {}
                prev = None
                for fn, waits, inc, n in evs:
                    for s, v in waits:
                        if v > 0 and seen.get(s, 0) < v:
                            eng.wait_ge(sems[s], v)
                            seen[s] = v
                    if chain and n == 1 and prev is not None:
                        s, v = prev
                        if seen.get(s, 0) < v:
                            eng.wait_ge(sems[s], v)
                            seen[s] = v
                    ins = fn(eng)
                    use = inc if inc else (chain if (chain and n == 1) else None)
                    if use:
                        ins.then_inc(sems[use], n)
                        cnt[use] = cnt.get(use, 0) + n
                        if chain and n == 1:
                            prev = (use, cnt[use])
            return run

        block.sync(body("sync"))
        block.vector(body("vector"))
        block.scalar(body("scalar"))
        block.tensor(body("tensor"))
        block.gpsimd(body("gpsimd"))
    return nc


_BF = ml_dtypes.bfloat16


def _host_inputs(x, W, w1_w, w1_s, w2_w, w2_s, h1_w, h1_s, h2_w, h2_s, w3_w, w3_s):
    x = np.asarray(x, np.float32).reshape(SEQ, D)
    W = np.asarray(W, np.float32)
    shared = {
        "vmask": np.where((np.arange(SEQ)[None, :] // 128)
                          >= np.arange(128)[:, None], NEG, 0.0).astype(_BF),
        "tri": np.where(np.arange(128)[None, :] <= np.arange(128)[:, None],
                        0.0, NEG).astype(_BF),
        "idn": np.eye(128).astype(_BF),
        "wkT": np.ascontiguousarray(W[:D].T),
        "wvT": np.ascontiguousarray(W[D:].T),
        "w1T": np.ascontiguousarray(np.asarray(w1_w, np.float32).T),
        "w2T": np.ascontiguousarray(np.asarray(w2_w, np.float32).T),
        "h1T": np.ascontiguousarray(np.asarray(h1_w, np.float32).T),
        "h2T": np.ascontiguousarray(np.asarray(h2_w, np.float32).T),
        "w3T": np.ascontiguousarray(np.asarray(w3_w, np.float32).T),
        "s1i": np.ascontiguousarray(np.asarray(w1_s, np.float32).reshape(32, 128).T),
        "s2i": np.ascontiguousarray(np.asarray(w2_s, np.float32).reshape(32, 128).T),
        "sh1i": np.ascontiguousarray(np.asarray(h1_s, np.float32).reshape(32, 128).T),
        "sh2i": np.ascontiguousarray(np.asarray(h2_s, np.float32).reshape(32, 128).T),
        "s3i": np.ascontiguousarray(np.asarray(w3_s, np.float32).reshape(8, 128).T),
        "epsi": np.full((128, 1), 1e-5, np.float32),
    }
    maps = []
    for c in range(NCORES):
        tiles = [8 * s + c for s in range(NT)]
        rows = np.concatenate([np.arange(g * 128, (g + 1) * 128) for g in tiles])
        xq_h = np.ascontiguousarray(x[rows])
        u = np.zeros((128, 2048), np.float32)
        for s, g in enumerate(tiles):
            u[g, s * 128:(s + 1) * 128] = 1.0
        maps.append(dict(shared,
                         xqT=np.ascontiguousarray(xq_h.T),
                         umask=u.astype(_BF)))
    return maps


def kernel(L0_attn_x, L0_attn_W, L0_ffn_w1_w, L0_ffn_w1_s, L0_ffn_w2_w,
           L0_ffn_w2_s, L0_ffn_h1_w, L0_ffn_h1_s, L0_ffn_h2_w, L0_ffn_h2_s,
           L0_ffn_w3_w, L0_ffn_w3_s):
    maps = _host_inputs(L0_attn_x, L0_attn_W, L0_ffn_w1_w, L0_ffn_w1_s,
                        L0_ffn_w2_w, L0_ffn_w2_s, L0_ffn_h1_w, L0_ffn_h1_s,
                        L0_ffn_h2_w, L0_ffn_h2_s, L0_ffn_w3_w, L0_ffn_w3_s)
    nc = build_nc()
    res = run_bass_kernel_spmd(nc, maps, core_ids=list(range(NCORES)))
    full = np.empty((SEQ, D), np.float32)
    for c in range(NCORES):
        o = res.results[c]["out"]
        for s in range(NT):
            g = 8 * s + c
            full[g * 128:(g + 1) * 128] = o[s * 128:(s + 1) * 128]
    return full.reshape(1, SEQ, D)
